# version 01
# baseline (speedup 1.0000x reference)
import numpy as np
import ml_dtypes
from contextlib import ExitStack
import concourse.bass as bass
import concourse.mybir as mybir
from concourse.bass_utils import run_bass_kernel_spmd

F32 = mybir.dt.float32
BF16 = mybir.dt.bfloat16
AX = mybir.AxisListType.X
OP = mybir.AluOpType
AF = mybir.ActivationFunctionType

B, S, D = 2, 4096, 2048
H, DH = 16, 128
DC, DQC = 512, 1024
HID = 8192
NT = 8
NTOK = NT * 128
ST = S // 128
CH = 512
NCH = S // CH
MAGIC = 3 * 2.0 ** 22
SM_SCALE = 1.0 / 16.0
LN_EPS = 1e-5
BF = ml_dtypes.bfloat16

ENGS = ("tensor", "vector", "scalar", "gpsimd", "sync")


class Flow:
    __slots__ = ("sem", "count", "name")

    def __init__(self, sem, name=""):
        self.sem = sem
        self.count = 0
        self.name = name

    def at(self):
        return (self, self.count)


class Rec:
    def __init__(self, nc):
        self.nc = nc
        self.lists = {k: [] for k in ENGS}
        self.last_wait = {}

    def emit(self, eng, fn, waits=(), inc=None, inc_n=1):
        wl = []
        for w in waits:
            if w is None:
                continue
            fl, val = w
            if val <= 0:
                continue
            key = (eng, id(fl))
            if self.last_wait.get(key, -1) >= val:
                continue
            self.last_wait[key] = val
            wl.append((fl.sem, val))
        sem = None
        if inc is not None:
            inc.count += inc_n
            sem = inc.sem

        def thunk(e, wl=wl, fn=fn, sem=sem, n=inc_n):
            for s, v in wl:
                e.wait_ge(s, v)
            instr = fn(e)
            if sem is not None:
                instr.then_inc(sem, n)

        self.lists[eng].append(thunk)

    def replay(self, block):
        for eng in ENGS:
            lst = self.lists[eng]
            getattr(block, eng)(lambda e, lst=lst: [f(e) for f in lst])


class SBAlloc:
    def __init__(self, nc):
        self.nc = nc
        self.stack = []
        self.serial = 0

    def alloc(self, name, shape, dt):
        self.serial += 1
        g = self.nc.sbuf_tensor(f"{name}__{self.serial}", list(shape), dt)
        h = g.__enter__()
        self.stack.append((name, g))
        return h

    def free(self, *names):
        want = set(names)
        for _ in range(len(want)):
            n, g = self.stack.pop()
            assert n in want, f"free order violation: {n} vs {want}"
            want.discard(n)
            g.__exit__(None, None, None)


def build_program(debug=(), no_ffn=False, ablate=()):
    nc = bass.Bass("TRN2", target_bir_lowering=False)
    nc.sbuf_top = 199680
    dbg = set(debug)

    def din(name, shape, dt=F32):
        return nc.dram_tensor(name, list(shape), dt, kind="ExternalInput").ap()

    hT = din("hT", [D, S])
    hTown = din("hTown", [D, NTOK])
    h_own = din("h_own", [NTOK, D])
    wdkvT = din("wdkvT", [D, DC])
    bdkv_col = din("bdkv_col", [128, DC // 128])
    wdqT = din("wdqT", [D, DQC])
    bdq_col = din("bdq_col", [128, DQC // 128])
    wuqT = din("wuqT", [DQC, D])
    buq_bc = din("buq_bc", [128, D])
    wukT = din("wukT", [DC, D])
    buk_bc = din("buk_bc", [128, D])
    wuvT = din("wuvT", [DC, D])
    buv_bc = din("buv_bc", [128, D])
    wkrT = din("wkrT", [D, D])
    bkr_bc = din("bkr_bc", [128, D])
    woT = din("woT", [D, D])
    bo_bc = din("bo_bc", [128, D])
    wgT = din("wgT", [D, HID])
    bg_bc = din("bg_bc", [128, HID])
    wvT = din("wvT", [D, HID])
    bv_bc = din("bv_bc", [128, HID])
    woutT = din("woutT", [HID, D])
    bout_bc = din("bout_bc", [128, D])
    cosT = din("cosT", [S, 1024], BF16)
    sinT = din("sinT", [S, 1024], BF16)
    cos_own = din("cos_own", [NTOK, 1024], BF16)
    sin_own = din("sin_own", [NTOK, 1024], BF16)
    identI = din("identI", [128, 128], BF16)
    maskI = din("maskI", [128, CH], BF16)

    OUT = nc.dram_tensor("out", [NTOK, D], F32, kind="ExternalOutput").ap()

    def scrt(name, shape, dt=BF16):
        return nc.dram_tensor(name, list(shape), dt, kind="Internal").ap()

    KCs = scrt("KCs", [S, D])
    KRs = scrt("KRs", [S, D])
    Vs = scrt("Vs", [S, D])
    X1s = scrt("X1s", [NTOK, D], F32)
    TgS = scrt("TgS", [D, HID])
    TvS = scrt("TvS", [D, HID])
    ToS = scrt("ToS", [HID, D])

    dbg_out = {}
    if dbg:
        shapes = {
            "ckv": [DC, S], "cq": [DQC, NTOK], "qc": [NTOK, D],
            "qr": [NTOK, D], "kr": [S, D], "kc": [S, D], "v": [S, D],
            "gam": [128, 12], "o": [NTOK, D], "x1": [NTOK, D],
            "tg": [D, HID], "lacc": [128, NT * H], "oacc": [NTOK, D],
            "r1": [NTOK, D], "ot": [NTOK, D], "x1s": [NTOK, D],
            "lnx": [128, NT * 8], "wod": [128, 32768],
        }
        for k in dbg:
            if k in shapes:
                dbg_out[k] = nc.dram_tensor("dbg_" + k, shapes[k], F32,
                                            kind="ExternalOutput").ap()

    rec = Rec(nc)
    sb = SBAlloc(nc)
    ctx = ExitStack()

    flows = {}

    def flow(name):
        if name not in flows:
            sem = ctx.enter_context(nc.semaphore(name))
            flows[name] = Flow(sem, name)
        return flows[name]

    boot = flow("boot")
    f_ld = flow("f_ld")
    f_g = flow("f_g")
    f_sp = flow("f_sp")
    f_st = flow("f_st")
    c_t = flow("c_t")
    c_v = flow("c_v")
    c_s = flow("c_s")

    pbank = [ctx.enter_context(nc.psum_tensor(f"pb{i}", [128, 512], F32))
             for i in range(4)]
    psT = [ctx.enter_context(nc.psum_tensor(f"psT{i}", [128, 1024], BF16))
           for i in range(2)]

    def dma(queue, out, in_, fl, waits=()):
        rec.emit(queue, lambda e, o=out, i=in_: e.dma_start(out=o, in_=i),
                 waits=waits, inc=fl, inc_n=16)

    def dmaT(out, in_, fl, waits=()):
        rec.emit("sync",
                 lambda e, o=out, i=in_: e.dma_start(out=o, in_=i,
                                                     transpose=True),
                 waits=waits, inc=fl, inc_n=16)

    def mm(out, lhsT, rhs, start, stop, waits=(), inc=None):
        rec.emit("tensor",
                 lambda e, o=out, l=lhsT, r=rhs, a=start, b=stop:
                 e.matmul(o, l, r, start=a, stop=b),
                 waits=waits, inc=inc)

    def transp(out, in_, waits=(), inc=None):
        rec.emit("tensor",
                 lambda e, o=out, i=in_: e.transpose(o, i, ident_sb[:]),
                 waits=waits, inc=inc)

    def vop(fn, waits=()):
        rec.emit("vector", fn, waits=waits, inc=c_v)

    def sop(fn, waits=()):
        rec.emit("scalar", fn, waits=waits, inc=c_s)

    def fence():
        return [c_t.at(), c_v.at(), c_s.at()]

    bank_rd = {}

    SEM_LO, SEM_HI = 155, 256
    for _eng in ENGS:
        rec.emit(_eng, lambda e: e.sem_clear(range(SEM_LO, SEM_HI)))

    ident_sb = sb.alloc("ident_sb", [128, 128], BF16)
    mask_sb = sb.alloc("mask_sb", [128, CH], BF16)
    epsc = sb.alloc("epsc", [128, 1], F32)
    onescol = sb.alloc("onescol", [128, 1], F32)
    onesrow = sb.alloc("onesrow", [1, 128], F32)
    onesrow_bf = sb.alloc("onesrow_bf", [1, 128], BF16)
    s3row = sb.alloc("s3row", [1, 128], BF16)
    gam = sb.alloc("gam", [128, 12], F32)
    lnc = sb.alloc("lnc", [128, 8], F32)

    f_const = flow("f_const")
    dma("sync", ident_sb[:], identI[:], f_const)
    dma("sync", mask_sb[:], maskI[:], f_const)
    vop(lambda e: e.memset(epsc[:], LN_EPS))
    vop(lambda e: e.memset(onescol[:], 1.0))
    vop(lambda e: e.memset(onesrow[:], 1.0))
    vop(lambda e: e.memset(onesrow_bf[:], 1.0))
    init_mark = c_v.at()
    ld0 = f_const.at()

    arena_b = sb.alloc("arena_b", [128, 32768], BF16)
    qCT = arena_b[:, 0:16384].rearrange("p (h t) -> p h t", h=H)
    qRT = arena_b[:, 16384:32768].rearrange("p (h t) -> p h t", h=H)
    wo_view = arena_b[:, 0:32768].rearrange("p (a n) -> p a n", a=16)
    h3T = arena_b[:, 0:32768].rearrange("p (i a t) -> p i a t", i=4, a=64)
    ckvT = arena_b[:, 0:16384].rearrange("p (a t) -> p a t", a=DC // 128)

    WMATS = [(wgT, D, HID, TgS, 0), (wvT, D, HID, TvS, 1),
             (woutT, HID, D, ToS, 2)]
    gbuf = [sb.alloc(f"gbuf{i}", [128, 2048], F32) for i in range(2)]
    f_gb = [flow("f_gb0"), flow("f_gb1")]
    gacc = sb.alloc("gacc", [128, 4], F32)
    gtmp = sb.alloc("gtmp", [128, 2], F32)

    for wT, nr, ncol, _, m in WMATS:
        vop(lambda e, m=m: e.memset(gacc[:, m:m + 1], 0.0),
            waits=[c_t.at()])
        slot_rd = [None, None]
        idx = 0
        nch = (nr // 128) * (ncol // 2048)
        for a in range(nr // 128):
            for cb in range(ncol // 2048):
                sl = idx % 2
                dma("sync", gbuf[sl][:],
                    wT[a * 128:(a + 1) * 128, cb * 2048:(cb + 1) * 2048],
                    f_gb[sl], waits=[slot_rd[sl]])
                lm = f_gb[sl].at()
                vop(lambda e, sl=sl: e.reduce_sum(
                    gtmp[:, sl:sl + 1], gbuf[sl][:], axis=AX,
                    apply_absolute_value=True), waits=[lm])
                if idx > 0:
                    psl = (idx - 1) % 2
                    vop(lambda e, psl=psl, m=m: e.tensor_add(
                        gacc[:, m:m + 1], gacc[:, m:m + 1],
                        gtmp[:, psl:psl + 1]))
                slot_rd[sl] = c_v.at()
                idx += 1
        vop(lambda e: e.tensor_copy(gtmp[:, 0:1], gtmp[:, 0:1]))
        vop(lambda e, m=m: e.tensor_add(
            gacc[:, m:m + 1], gacc[:, m:m + 1],
            gtmp[:, (nch - 1) % 2:(nch - 1) % 2 + 1]))
        mm(pbank[3][0:1, m:m + 1], gacc[:, m:m + 1], onescol[:],
           start=True, stop=True,
           waits=[c_v.at(), init_mark, bank_rd.get(id(pbank[3]))], inc=c_t)
        vop(lambda e, m=m, n=float(nr * ncol): e.tensor_scalar(
            gam[0:1, 9 + m:10 + m], pbank[3][0:1, m:m + 1],
            1.0 / n, 1e-5, OP.mult, OP.max), waits=[c_t.at()])
        bank_rd[id(pbank[3])] = c_v.at()
        mm(pbank[3][:, 256 + m:257 + m], onesrow[:], gam[0:1, 9 + m:10 + m],
           start=True, stop=True, waits=[c_v.at()], inc=c_t)
        vop(lambda e, m=m: e.tensor_copy(gam[:, m:m + 1],
                                         pbank[3][:, 256 + m:257 + m]),
            waits=[c_t.at()])
        bank_rd[id(pbank[3])] = c_v.at()
        vop(lambda e: e.tensor_copy(lnc[:, 6:7], lnc[:, 6:7]))
        vop(lambda e, m=m: e.reciprocal(gam[:, 3 + m:4 + m], gam[:, m:m + 1]))
        vop(lambda e, m=m: e.tensor_scalar(gam[:, 6 + m:7 + m],
                                           gam[:, m:m + 1],
                                           1.0 / 127.0, None, OP.mult))
    gam_mark = c_v.at()
    if "gam" in dbg:
        vop(lambda e: e.tensor_copy(gam[:, 9:12], gacc[:, 0:3]),
            waits=[gam_mark, c_t.at()])
        dma("sync", dbg_out["gam"][:], gam[:], f_st, waits=[c_v.at()])

    qout = [sb.alloc(f"qout{i}", [128, 2048], BF16) for i in range(2)]
    f_tw = [flow("f_tw0"), flow("f_tw1")]
    for wT, nr, ncol, Tdst, m in WMATS:
        slot_rd = [None, None]
        out_wr = [None, None]
        idx = 0
        for a in range(nr // 128):
            for cb in range(ncol // 2048):
                sl = idx % 2
                dma("sync", gbuf[sl][:],
                    wT[a * 128:(a + 1) * 128, cb * 2048:(cb + 1) * 2048],
                    f_gb[sl], waits=[slot_rd[sl], gam_mark])
                lm = f_gb[sl].at()
                vop(lambda e, sl=sl, m=m: e.tensor_scalar(
                    gbuf[sl][:], gbuf[sl][:], gam[:, 3 + m:4 + m], MAGIC,
                    OP.mult, OP.add), waits=[lm, out_wr[sl]])
                vop(lambda e, sl=sl: e.tensor_scalar(
                    gbuf[sl][:], gbuf[sl][:], MAGIC + 1.0, MAGIC - 1.0,
                    OP.min, OP.max))
                vop(lambda e, sl=sl: e.tensor_scalar(
                    qout[sl][:], gbuf[sl][:], MAGIC, None, OP.subtract))
                slot_rd[sl] = c_v.at()
                dma("sync",
                    Tdst[a * 128:(a + 1) * 128, cb * 2048:(cb + 1) * 2048],
                    qout[sl][:], f_tw[sl], waits=[slot_rd[sl]])
                out_wr[sl] = f_tw[sl].at()
                idx += 1
    quant_done_sp = [f_tw[0].at(), f_tw[1].at()]
    if "tg" in dbg:
        dma("gpsimd", dbg_out["tg"][:], TgS[:], f_st, waits=quant_done_sp)
    sb.free("qout1", "qout0", "gtmp", "gacc", "gbuf1", "gbuf0")


    def proj_feature_major(wT_d, ndcb, bcol_d, src_d, ntot, dst, dbg_key):
        CCH = 256
        w_sb = sb.alloc("pf_w", [128, 16, 128 * ndcb], BF16)
        b_sb = sb.alloc("pf_b", [128, ndcb], F32)
        htb = [sb.alloc(f"pf_ht{i}", [128, 16, CCH], BF16) for i in range(2)]
        f_pfw = flow(f"f_pfw_{dbg_key}")
        f_pfb = flow(f"f_pfb_{dbg_key}")
        f_pfh = [flow(f"f_pfh0_{dbg_key}"), flow(f"f_pfh1_{dbg_key}")]
        dma("gpsimd", w_sb[:], wT_d[:].rearrange("(a p) n -> p a n", p=128),
            f_pfw, waits=fence())
        dma("sync", b_sb[:], bcol_d[:], f_pfb, waits=fence())
        wm = f_pfw.at()
        bm = f_pfb.at()
        slot_rd = [None, None]
        for cc in range(ntot // CCH):
            sl = cc % 2
            dma("gpsimd", htb[sl][:],
                src_d[:, cc * CCH:(cc + 1) * CCH].rearrange(
                    "(a p) t -> p a t", p=128), f_pfh[sl],
                waits=[slot_rd[sl]])
            lm = f_pfh[sl].at()
            for dcb in range(ndcb):
                bk = pbank[dcb % 2]
                for a in range(16):
                    mw = [lm, wm, bank_rd.get(id(bk))] if a == 0 else []
                    mm(bk[:, 0:CCH], w_sb[:, a, dcb * 128:(dcb + 1) * 128],
                       htb[sl][:, a, :], start=(a == 0), stop=(a == 15),
                       waits=mw, inc=(c_t if a == 15 else None))
                tm = c_t.at()
                vop(lambda e, bk=bk, dcb=dcb, cc=cc: e.tensor_scalar(
                    dst[:, dcb, cc * CCH:(cc + 1) * CCH], bk[:, 0:CCH],
                    b_sb[:, dcb:dcb + 1], None, OP.add), waits=[tm, bm])
                bank_rd[id(bk)] = c_v.at()
            slot_rd[sl] = c_t.at()
        if dbg_key in dbg:
            for dcb in range(ndcb):
                dma("gpsimd", dbg_out[dbg_key][dcb * 128:(dcb + 1) * 128, :],
                    dst[:, dcb, :], f_st, waits=[c_v.at()])
        sb.free("pf_ht1", "pf_ht0", "pf_b", "pf_w")

    proj_feature_major(wdkvT, DC // 128, bdkv_col, hT, S, ckvT, "ckv")

    def proj_token_major_kv(wT_d, b_bc_d, DSTs, dbg_key):
        w_sb = sb.alloc("kv_w", [128, DC // 128, D], BF16)
        b_sb = sb.alloc("kv_b", [128, D], F32)
        out_t = sb.alloc("kv_t", [128, D], BF16)
        f_kvw = flow(f"f_kvw_{dbg_key}")
        f_kvb = flow(f"f_kvb_{dbg_key}")
        dma("gpsimd", w_sb[:], wT_d[:].rearrange("(a p) n -> p a n", p=128),
            f_kvw, waits=fence())
        dma("sync", b_sb[:], b_bc_d[:], f_kvb, waits=fence())
        wm = f_kvw.at()
        bm = f_kvb.at()
        spill_last = None
        for st in range(ST):
            for hb in range(4):
                bk = pbank[hb % 4]
                for dcb in range(DC // 128):
                    mw = [wm, bank_rd.get(id(bk))] if dcb == 0 else []
                    mm(bk[:], ckvT[:, dcb, st * 128:(st + 1) * 128],
                       w_sb[:, dcb, hb * 512:(hb + 1) * 512],
                       start=(dcb == 0), stop=(dcb == DC // 128 - 1),
                       waits=mw, inc=(c_t if dcb == DC // 128 - 1 else None))
                tm = c_t.at()
                vop(lambda e, bk=bk, hb=hb: e.tensor_tensor(
                    out_t[:, hb * 512:(hb + 1) * 512], bk[:],
                    b_sb[:, hb * 512:(hb + 1) * 512], OP.add),
                    waits=[tm, bm, spill_last])
                bank_rd[id(bk)] = c_v.at()
            dma("sync", DSTs[st * 128:(st + 1) * 128, :], out_t[:], f_sp,
                waits=[c_v.at()])
            spill_last = f_sp.at()
        if dbg_key in dbg:
            dma("gpsimd", dbg_out[dbg_key][:], DSTs[:], f_st,
                waits=[f_sp.at()])
        sb.free("kv_t", "kv_b", "kv_w")
        return f_sp.at()

    proj_token_major_kv(wukT, buk_bc, KCs, "kc")
    kv_spill_mark = proj_token_major_kv(wuvT, buv_bc, Vs, "v")

    cqT = sb.alloc("cqT", [128, DQC // 128, NTOK], BF16)
    proj_feature_major(wdqT, DQC // 128, bdq_col, hTown, NTOK, cqT, "cq")

    def rope_tile(dst, src, cbuf, sbuf_, tmp, ncols, waits):
        n2 = 2 * ncols
        vop(lambda e: e.tensor_mul(dst[:, 0:n2:2], src[:, 0:n2:2], cbuf[:]),
            waits=waits)
        vop(lambda e: e.tensor_mul(tmp[:], src[:, 1:n2:2], sbuf_[:]))
        vop(lambda e: e.tensor_sub(dst[:, 0:n2:2], dst[:, 0:n2:2], tmp[:]))
        vop(lambda e: e.tensor_mul(dst[:, 1:n2:2], src[:, 0:n2:2], sbuf_[:]))
        vop(lambda e: e.tensor_mul(tmp[:], src[:, 1:n2:2], cbuf[:]))
        vop(lambda e: e.tensor_add(dst[:, 1:n2:2], dst[:, 1:n2:2], tmp[:]))

    psT_rd = [None, None]

    def transpose_block(dst_fn, src, nchunks, waits):
        for a in range(nchunks):
            tb = psT[a % 2]
            transp(tb[:, 0:128], src[:, a * 128:(a + 1) * 128],
                   waits=list(waits) + [psT_rd[a % 2]], inc=c_t)
            tm = c_t.at()
            vop(lambda e, tb=tb, a=a: e.tensor_copy(dst_fn(a), tb[:, 0:128]),
                waits=[tm])
            psT_rd[a % 2] = c_v.at()
        return c_v.at()

    wuq_sb = sb.alloc("wuq_sb", [128, DQC // 128, D], BF16)
    buq_sb = sb.alloc("buq_sb", [128, D], F32)
    qc_t = sb.alloc("qc_t", [128, D], BF16)
    qr_t = sb.alloc("qr_t", [128, D], BF16)
    rtmp = sb.alloc("rtmp", [128, 1024], BF16)
    csb = [sb.alloc(f"csb{i}", [128, 1024], BF16) for i in range(2)]
    ssb = [sb.alloc(f"ssb{i}", [128, 1024], BF16) for i in range(2)]
    f_q3w = flow("f_q3w")
    f_q3b = flow("f_q3b")
    f_cs = [flow("f_cs0"), flow("f_cs1")]
    dma("gpsimd", wuq_sb[:], wuqT[:].rearrange("(a p) n -> p a n", p=128),
        f_q3w, waits=fence())
    dma("sync", buq_sb[:], buq_bc[:], f_q3b, waits=fence())
    wm = f_q3w.at()
    bm = f_q3b.at()
    cs_rd = [None, None]
    qtr_prev = None
    qdbg_prev = None
    for ot in range(NT):
        sl = ot % 2
        dma("sync", csb[sl][:], cos_own[ot * 128:(ot + 1) * 128, :],
            f_cs[sl], waits=[cs_rd[sl]])
        dma("sync", ssb[sl][:], sin_own[ot * 128:(ot + 1) * 128, :],
            f_cs[sl])
        csm = f_cs[sl].at()
        for hb in range(4):
            bk = pbank[hb % 2]
            for qcb in range(DQC // 128):
                mw = [wm, bank_rd.get(id(bk))] if qcb == 0 else []
                mm(bk[:], cqT[:, qcb, ot * 128:(ot + 1) * 128],
                   wuq_sb[:, qcb, hb * 512:(hb + 1) * 512],
                   start=(qcb == 0), stop=(qcb == DQC // 128 - 1),
                   waits=mw, inc=(c_t if qcb == DQC // 128 - 1 else None))
            tm = c_t.at()
            vop(lambda e, bk=bk, hb=hb: e.tensor_tensor(
                qc_t[:, hb * 512:(hb + 1) * 512], bk[:],
                buq_sb[:, hb * 512:(hb + 1) * 512], OP.add),
                waits=[tm, bm, qtr_prev, qdbg_prev])
            bank_rd[id(bk)] = c_v.at()
        qc_mark = c_v.at()
        rope_tile(qr_t, qc_t, csb[sl], ssb[sl], rtmp, 1024,
                  [qc_mark, csm, qtr_prev, qdbg_prev])
        rope_mark = c_v.at()
        cs_rd[sl] = rope_mark
        transpose_block(lambda a, ot=ot: qCT[:, a, ot * 128:(ot + 1) * 128],
                        qc_t, 16, [qc_mark, ld0])
        transpose_block(lambda a, ot=ot: qRT[:, a, ot * 128:(ot + 1) * 128],
                        qr_t, 16, [rope_mark])
        qtr_prev = c_t.at()
        if "qc" in dbg:
            dma("gpsimd", dbg_out["qc"][ot * 128:(ot + 1) * 128, :], qc_t[:],
                f_st, waits=[qc_mark])
        if "qr" in dbg:
            dma("gpsimd", dbg_out["qr"][ot * 128:(ot + 1) * 128, :], qr_t[:],
                f_st, waits=[rope_mark])
        if "qc" in dbg or "qr" in dbg:
            qdbg_prev = f_st.at()
    sb.free("ssb1", "ssb0", "csb1", "csb0", "rtmp", "qr_t", "qc_t",
            "buq_sb", "wuq_sb", "cqT")

    wkr_sb = sb.alloc("wkr_sb", [128, 16, 1024], BF16)
    bkr_sb = sb.alloc("bkr_sb", [128, D], F32)
    kr_t = sb.alloc("kr_t", [128, 1024], BF16)
    krr_t = sb.alloc("krr_t", [128, 1024], BF16)
    rtmp = sb.alloc("rtmp", [128, 512], BF16)
    csb = [sb.alloc(f"csb{i}", [128, 512], BF16) for i in range(2)]
    ssb = [sb.alloc(f"ssb{i}", [128, 512], BF16) for i in range(2)]
    htb = [sb.alloc(f"htb{i}", [128, 16, 128], BF16) for i in range(2)]
    f_p4w = flow("f_p4w")
    f_p4b = flow("f_p4b")
    f_p4h = [flow("f_p4h0"), flow("f_p4h1")]
    f_p4c = [flow("f_p4c0"), flow("f_p4c1")]
    dma("sync", bkr_sb[:], bkr_bc[:], f_p4b, waits=fence())
    bm = f_p4b.at()
    for half in range(2):
        dma("gpsimd", wkr_sb[:],
            wkrT[:, half * 1024:(half + 1) * 1024].rearrange(
                "(a p) n -> p a n", p=128), f_p4w, waits=fence())
        wm = f_p4w.at()
        slot_rd = [None, None]
        cs_rd = [None, None]
        spill_last = None
        for st in range(ST):
            sl = st % 2
            dma("gpsimd", htb[sl][:],
                hT[:, st * 128:(st + 1) * 128].rearrange(
                    "(a p) t -> p a t", p=128), f_p4h[sl],
                waits=[slot_rd[sl]])
            lm = f_p4h[sl].at()
            dma("sync", csb[sl][:],
                cosT[st * 128:(st + 1) * 128, half * 512:(half + 1) * 512],
                f_p4c[sl], waits=[cs_rd[sl]])
            dma("sync", ssb[sl][:],
                sinT[st * 128:(st + 1) * 128, half * 512:(half + 1) * 512],
                f_p4c[sl])
            csm = f_p4c[sl].at()
            for hb in range(2):
                bk = pbank[hb % 2]
                for a in range(16):
                    mw = [lm, wm, bank_rd.get(id(bk))] if a == 0 else []
                    mm(bk[:], htb[sl][:, a, :],
                       wkr_sb[:, a, hb * 512:(hb + 1) * 512],
                       start=(a == 0), stop=(a == 15),
                       waits=mw, inc=(c_t if a == 15 else None))
                tm = c_t.at()
                vop(lambda e, bk=bk, hb=hb, half=half: e.tensor_tensor(
                    kr_t[:, hb * 512:(hb + 1) * 512], bk[:],
                    bkr_sb[:, half * 1024 + hb * 512:
                           half * 1024 + (hb + 1) * 512], OP.add),
                    waits=[tm, bm])
                bank_rd[id(bk)] = c_v.at()
            kr_mark = c_v.at()
            rope_tile(krr_t, kr_t, csb[sl], ssb[sl], rtmp, 512,
                      [kr_mark, csm, spill_last])
            rope_mark = c_v.at()
            cs_rd[sl] = rope_mark
            slot_rd[sl] = c_t.at()
            dma("sync", KRs[st * 128:(st + 1) * 128,
                            half * 1024:(half + 1) * 1024], krr_t[:], f_sp,
                waits=[rope_mark])
            spill_last = f_sp.at()
    kr_spill_mark = f_sp.at()
    if "kr" in dbg:
        dma("gpsimd", dbg_out["kr"][:], KRs[:], f_st, waits=[kr_spill_mark])
    sb.free("htb1", "htb0", "ssb1", "ssb0", "csb1", "csb0", "rtmp",
            "krr_t", "kr_t", "bkr_sb", "wkr_sb")

    arena_f = sb.alloc("arena_f", [128, 16512], F32)
    O_acc = arena_f[:, 0:16384].rearrange("p (a d) -> p a d", a=NT)
    lacc = arena_f[:, 16384:16512].rearrange("p (a h) -> p a h", a=NT)
    x1g = arena_f[:, 0:8192].rearrange("p (a d) -> p a d", a=4)
    r2g = arena_f[:, 8192:16384].rearrange("p (a d) -> p a d", a=4)
    kctb = [sb.alloc(f"kctb{i}", [128, 8, CH], BF16) for i in range(2)]
    krtb = [sb.alloc(f"krtb{i}", [128, 8, CH], BF16) for i in range(2)]
    vtb_b = sb.alloc("vtb", [128, 4, 1024], BF16)
    vtb = [vtb_b, vtb_b]
    P_sb = [sb.alloc(f"P_sb{i}", [128, CH], BF16) for i in range(2)]
    PT_sb = [sb.alloc(f"PT_sb{i}", [128, CH], BF16) for i in range(2)]
    ltmp = sb.alloc("ltmp", [128, 2], F32)
    f_at = [flow("f_at0"), flow("f_at1")]

    fm = fence()
    unit = 0
    psS = [pbank[0], pbank[1]]
    psAV = [pbank[2], pbank[3]]
    exp_done = {}
    p_free = [None, None]
    l_free = [None, None]
    pt_free = [None, None]
    av_evac = {}
    ld_consumed = [None, None]
    for c in range(NCH):
        for hg in range(2):
            par = (2 * c + hg) % 2
            for h8 in range(8):
                h = hg * 8 + h8
                dmaT(kctb[par][:, h8, :],
                     KCs[c * CH:(c + 1) * CH, h * 128:(h + 1) * 128],
                     f_at[par], waits=[ld_consumed[par], kv_spill_mark,
                                       kr_spill_mark] + (fm if c == 0 else []))
                dmaT(krtb[par][:, h8, :],
                     KRs[c * CH:(c + 1) * CH, h * 128:(h + 1) * 128],
                     f_at[par])
            dma("sync", vtb[par][:],
                Vs[c * CH:(c + 1) * CH,
                   hg * 1024:(hg + 1) * 1024].rearrange(
                       "(s p) d -> p s d", p=128), f_at[par],
                waits=[ld_consumed[1 - par]])
            ldm = f_at[par].at()
            for h8 in range(8):
                h = hg * 8 + h8
                for t in range(c, NT):
                    us = unit % 2
                    sbk = psS[us]
                    mm(sbk[:], qCT[:, h, t * 128:(t + 1) * 128],
                       kctb[par][:, h8, :], start=True, stop=False,
                       waits=[ldm, exp_done.get(id(sbk))])
                    mm(sbk[:], qRT[:, h, t * 128:(t + 1) * 128],
                       krtb[par][:, h8, :], start=False, stop=True, inc=c_t)
                    sm = c_t.at()
                    sop(lambda e, us=us, sbk=sbk: e.activation(
                        P_sb[us][:], sbk[:], AF.Exp, scale=SM_SCALE,
                        accum_out=ltmp[:, us:us + 1]),
                        waits=[sm, p_free[us], l_free[us]])
                    exp_done[id(sbk)] = c_s.at()
                    em = c_s.at()
                    lw = [em]
                    if t == c:
                        vop(lambda e, us=us: e.tensor_mul(
                            P_sb[us][:], P_sb[us][:], mask_sb[:]),
                            waits=[em, ld0])
                        vop(lambda e, us=us: e.reduce_sum(
                            ltmp[:, us:us + 1], P_sb[us][:], axis=AX))
                        lw = [c_v.at()]
                    lvm = c_v.at()
                    tb = psT[us]
                    for s2 in range(4):
                        transp(tb[:, s2 * 128:(s2 + 1) * 128],
                               P_sb[us][:, s2 * 128:(s2 + 1) * 128],
                               waits=([em, lvm, psT_rd[us]] if s2 == 0
                                      else []),
                               inc=(c_t if s2 == 3 else None))
                    trm = c_t.at()
                    p_free[us] = trm
                    vop(lambda e, us=us, tb=tb: e.tensor_copy(
                        PT_sb[us][:], tb[:, 0:512]),
                        waits=[trm, pt_free[us]])
                    psT_rd[us] = c_v.at()
                    ptm = c_v.at()
                    if c == 0:
                        vop(lambda e, t=t, h=h, us=us: e.tensor_copy(
                            lacc[:, t, h:h + 1], ltmp[:, us:us + 1]),
                            waits=lw)
                    else:
                        vop(lambda e, t=t, h=h, us=us: e.tensor_add(
                            lacc[:, t, h:h + 1], lacc[:, t, h:h + 1],
                            ltmp[:, us:us + 1]), waits=lw)
                    l_free[us] = c_v.at()
                    av = psAV[us]
                    for s2 in range(4):
                        mw = [ptm, av_evac.get(id(av))] if s2 == 0 else []
                        mm(av[:, 0:128],
                           PT_sb[us][:, s2 * 128:(s2 + 1) * 128],
                           vtb[par][:, s2, h8 * 128:(h8 + 1) * 128],
                           start=(s2 == 0), stop=(s2 == 3),
                           waits=mw, inc=(c_t if s2 == 3 else None))
                    avm = c_t.at()
                    pt_free[us] = avm
                    if c == 0:
                        vop(lambda e, t=t, h=h, av=av: e.tensor_copy(
                            O_acc[:, t, h * 128:(h + 1) * 128], av[:, 0:128]),
                            waits=[avm])
                    else:
                        vop(lambda e, t=t, h=h, av=av: e.tensor_add(
                            O_acc[:, t, h * 128:(h + 1) * 128],
                            O_acc[:, t, h * 128:(h + 1) * 128], av[:, 0:128]),
                            waits=[avm])
                    av_evac[id(av)] = c_v.at()
                    unit += 1
            ld_consumed[par] = c_t.at()
    attn_done = fence()
    if "lacc" in dbg:
        dma("sync", dbg_out["lacc"][:],
            lacc[:].rearrange("p a h -> p (a h)"), f_st, waits=attn_done)
    if "oacc" in dbg:
        for t in range(NT):
            dma("sync", dbg_out["oacc"][t * 128:(t + 1) * 128, :],
                O_acc[:, t, :], f_st, waits=attn_done)
    sb.free("ltmp", "PT_sb1", "PT_sb0", "P_sb1", "P_sb0",
            "vtb", "krtb1", "krtb0", "kctb1", "kctb0")

    sqs = sb.alloc("sqs", [128, D // 2], F32)
    bo_sb = sb.alloc("bo_sb", [128, D], F32)
    hob = [sb.alloc(f"hob{i}", [128, D], F32) for i in range(2)]
    o_t = sb.alloc("o_t", [128, D], BF16)
    oT_sb = sb.alloc("oT_sb", [128, 16, 128], BF16)
    x1_t = sb.alloc("x1_t", [128, D], F32)
    linv = sb.alloc("linv", [128, H], F32)
    f_wo = flow("f_wo")
    f_ob = flow("f_ob")
    f_ho = [flow("f_ho0"), flow("f_ho1")]
    if "nowoload" not in ablate:
        dma("gpsimd", wo_view[:],
            woT[:].rearrange("(a p) n -> p a n", p=128),
            f_wo, waits=attn_done)
    dma("sync", bo_sb[:], bo_bc[:], f_ob, waits=attn_done)
    wm = f_wo.at()
    bm = f_ob.at()
    ho_rd = [None, None]
    ln_state = {"sqs_rd": None}
    if "lnx" in dbg:
        lnsave = sb.alloc("lnsave", [128, NT, 8], F32)

    def layernorm(x_ap, r_mark, ln_out=None, scratch=None):
        sq = scratch if scratch is not None else sqs
        rm = r_mark if isinstance(r_mark, list) else [r_mark]
        h_ = D // 2
        vop(lambda e: e.tensor_mul(sq[:], x_ap[:, 0:h_], x_ap[:, 0:h_]),
            waits=rm)
        vop(lambda e: e.reduce_sum(lnc[:, 0:1], x_ap, axis=AX))
        vop(lambda e: e.reduce_sum(lnc[:, 2:3], sq[:], axis=AX))
        vop(lambda e: e.tensor_mul(sq[:], x_ap[:, h_:D], x_ap[:, h_:D]))
        vop(lambda e: e.reduce_sum(lnc[:, 6:7], sq[:], axis=AX))
        vop(lambda e: e.tensor_scalar(lnc[:, 1:2], lnc[:, 0:1], 1.0 / D,
                                      None, OP.mult))
        vop(lambda e: e.tensor_copy(epsc[:], epsc[:]))
        vop(lambda e: e.tensor_add(lnc[:, 2:3], lnc[:, 2:3], lnc[:, 6:7]))
        vop(lambda e: e.tensor_copy(epsc[:], epsc[:]))
        vop(lambda e: e.tensor_scalar(lnc[:, 3:4], lnc[:, 2:3], 1.0 / D,
                                      None, OP.mult))
        vop(lambda e: e.tensor_mul(lnc[:, 4:5], lnc[:, 1:2], lnc[:, 1:2]))
        vop(lambda e: e.tensor_copy(epsc[:], epsc[:]))
        vop(lambda e: e.tensor_sub(lnc[:, 3:4], lnc[:, 3:4], lnc[:, 4:5]))
        vop(lambda e: e.tensor_copy(epsc[:], epsc[:]))
        sop(lambda e: e.activation(lnc[:, 4:5], lnc[:, 3:4], AF.Sqrt,
                                   bias=epsc[:, 0:1]),
            waits=[c_v.at(), init_mark])
        vop(lambda e: e.tensor_copy(lnc[:, 6:7], lnc[:, 6:7]),
            waits=[c_s.at()])
        vop(lambda e: e.reciprocal(lnc[:, 5:6], lnc[:, 4:5]))
        vop(lambda e: e.tensor_copy(lnc[:, 7:8], lnc[:, 6:7]))
        vop(lambda e: e.tensor_scalar(ln_out if ln_out is not None else x_ap,
                                      x_ap, lnc[:, 1:2], lnc[:, 5:6],
                                      OP.subtract, OP.mult))
        return c_v.at()

    if "can" in dbg:
        can_words = (nc.sbuf_top - nc.sbuf_base) // 4 - 64
        canary = sb.alloc("canary", [128, can_words], F32)
        dbg_out["can"] = nc.dram_tensor("dbg_can", [128, can_words], F32,
                                        kind="ExternalOutput").ap()
        dbg_out["canbase"] = nc.dram_tensor("dbg_canbase", [1, 1], F32,
                                            kind="ExternalOutput").ap()
        can_base_addr = nc.sbuf_base - can_words * 4
        dbg_out["can2"] = nc.dram_tensor("dbg_can2", [128, can_words], F32,
                                         kind="ExternalOutput").ap()
        vop(lambda e: e.memset(canary[:], 1234.5), waits=fence())
        can_set = c_v.at()
        dma("sync", dbg_out["can2"][:], canary[:], f_st, waits=[can_set])
    otr_prev = None
    x1sp_prev = None
    x1dbg_prev = None
    tile_fence = []
    for t in range(NT):
        sl = t % 2
        dma("sync", hob[sl][:], h_own[t * 128:(t + 1) * 128, :], f_ho[sl],
            waits=[ho_rd[sl]] + (attn_done if t == 0 else []))
        hm = f_ho[sl].at()
        if "nowo" in ablate:
            for db in range(4):
                vop(lambda e, db=db, sl=sl: e.tensor_tensor(
                    x1_t[:, db * 512:(db + 1) * 512],
                    bo_sb[:, db * 512:(db + 1) * 512],
                    hob[sl][:, db * 512:(db + 1) * 512], OP.add),
                    waits=[hm, bm, x1sp_prev] + (attn_done if t == 0 else []))
            ho_rd[sl] = c_v.at()
            r_marks = [c_v.at()]
            x1m = layernorm(x1_t[:], r_marks) if "noln" not in ablate \
                else r_marks[0]
            dma("sync", X1s[t * 128:(t + 1) * 128, :], x1_t[:], f_sp,
                waits=[x1m])
            x1sp_prev = f_sp.at()
            if "can" in dbg:
                key = f"can3_{t}"
                dbg_out[key] = nc.dram_tensor("dbg_" + key,
                                              [128, canary.shape[1]], F32,
                                              kind="ExternalOutput").ap()
                dma("sync", dbg_out[key][:], canary[:], f_st,
                    waits=[x1sp_prev, can_set])
            continue
        vop(lambda e, t=t: e.reciprocal(linv[:], lacc[:, t, :]),
            waits=(attn_done if t == 0 else []) +
                  (tile_fence if "serial7b" in ablate else []))
        vop(lambda e: e.tensor_copy(lnc[:, 6:7], lnc[:, 6:7]))
        for h in range(H):
            vop(lambda e, t=t, h=h: e.tensor_scalar(
                o_t[:, h * 128:(h + 1) * 128],
                O_acc[:, t, h * 128:(h + 1) * 128],
                linv[:, h:h + 1], None, OP.mult),
                waits=([otr_prev] if h == 0 else []))
        om = c_v.at()
        if "o" in dbg:
            dma("gpsimd", dbg_out["o"][t * 128:(t + 1) * 128, :], o_t[:],
                f_st, waits=[om])
        transpose_block(lambda a: oT_sb[:, a, :], o_t, 16,
                        [om] + ([f_st.at()] if "o" in dbg else []))
        otr_prev = c_t.at()
        otm = c_v.at()
        if "ot" in dbg:
            dma("gpsimd", dbg_out["ot"][t * 128:(t + 1) * 128, :],
                oT_sb[:].rearrange("p a t -> p (a t)"), f_st, waits=[otm])
            otm_l = [otm, f_st.at()]
        else:
            otm_l = [otm]
        for db in range(4):
            bk = pbank[db]
            for a in range(16):
                mw = (otm_l + [wm, bank_rd.get(id(bk)), av_evac.get(id(bk))]
                      + attn_done) if a == 0 else []
                mm(bk[:], oT_sb[:, a, :],
                   wo_view[:, a, db * 512:(db + 1) * 512],
                   start=(a == 0), stop=(a == 15),
                   waits=mw, inc=(c_t if a == 15 else None))
            tm = c_t.at()
            vop(lambda e, bk=bk, db=db: e.tensor_tensor(
                x1_t[:, db * 512:(db + 1) * 512], bk[:],
                bo_sb[:, db * 512:(db + 1) * 512], OP.add),
                waits=[tm, bm, x1sp_prev, x1dbg_prev])
            if "noh" not in ablate:
                vop(lambda e, db=db, sl=sl: e.tensor_add(
                    x1_t[:, db * 512:(db + 1) * 512],
                    x1_t[:, db * 512:(db + 1) * 512],
                    hob[sl][:, db * 512:(db + 1) * 512]), waits=[hm])
            bank_rd[id(bk)] = c_v.at()
        ho_rd[sl] = c_v.at()
        r_marks = [c_v.at()]
        if "r1" in dbg:
            dma("gpsimd", dbg_out["r1"][t * 128:(t + 1) * 128, :], x1_t[:],
                f_st, waits=[c_v.at()])
            r_marks.append(f_st.at())
        if "noln" in ablate:
            x1m = r_marks[0]
        elif "lnsq" in ablate:
            sop(lambda e: e.activation(sqs[:], x1_t[:], AF.Square),
                waits=r_marks)
            rec.emit("vector", lambda e: e.nop(), waits=[c_s.at()], inc=c_v)
            x1m = c_v.at()
        elif "lnsq2" in ablate:
            sop(lambda e: e.activation(x1o_t[:], x1_t[:], AF.Square),
                waits=r_marks)
            rec.emit("vector", lambda e: e.nop(), waits=[c_s.at()], inc=c_v)
            x1m = c_v.at()
        elif "lnsq3" in ablate:
            sop(lambda e: e.activation(sqs[:, 0:512], x1_t[:, 0:512],
                                       AF.Square), waits=r_marks)
            rec.emit("vector", lambda e: e.nop(), waits=[c_s.at()], inc=c_v)
            x1m = c_v.at()
        elif "lnrd" in ablate:
            vop(lambda e: e.reduce_sum(lnc[:, 0:1], x1_t[:], axis=AX),
                waits=r_marks)
            vop(lambda e: e.tensor_copy(lnc[:, 6:7], lnc[:, 6:7]))
            x1m = c_v.at()
        else:
            x1m = layernorm(x1_t[:], r_marks)
            if "lnx" in dbg:
                vop(lambda e, t=t: e.tensor_copy(lnsave[:, t, :], lnc[:]))
        dma("sync", X1s[t * 128:(t + 1) * 128, :], x1_t[:], f_sp,
            waits=[x1m])
        x1sp_prev = f_sp.at()
        if "can" in dbg and t == 0:
            dbg_out["can3"] = nc.dram_tensor("dbg_can3", canary.shape[1:2]
                                             and [128, canary.shape[1]], F32,
                                             kind="ExternalOutput").ap()
            dma("sync", dbg_out["can3"][:], canary[:], f_st,
                waits=[x1sp_prev, can_set])
        tile_fence = fence() + [x1sp_prev]
        if "x1" in dbg:
            dma("gpsimd", dbg_out["x1"][t * 128:(t + 1) * 128, :], x1_t[:],
                f_st, waits=[x1m])
            x1dbg_prev = f_st.at()
    x1_all = f_sp.at()
    if "wod" in dbg:
        dma("gpsimd", dbg_out["wod"][:], arena_b[:], f_st, waits=fence())
    if "lnx" in dbg:
        dma("sync", dbg_out["lnx"][:],
            lnsave[:].rearrange("p a b -> p (a b)"), f_st, waits=[c_v.at()])
    if "x1s" in dbg:
        dma("gpsimd", dbg_out["x1s"][:], X1s[:], f_st, waits=[x1_all])
    wo_done = fence()
    if "can" in dbg:
        sb.free("canary")
    if "lnx" in dbg:
        sb.free("lnsave")
    sb.free("linv", "x1_t", "oT_sb", "o_t", "hob1", "hob0",
            "bo_sb", "sqs")

    for k, shp, dt in (("xqt", [128, 8192], BF16), ("h3r", [128, 32768], BF16),
                       ("h3q", [128, 32768], BF16), ("r2", [128, 8192], F32),
                       ("sr8", [128, 24], F32)):
        if k in dbg:
            dbg_out[k] = nc.dram_tensor("dbg_" + k, shp,
                                        dt if k not in ("r2", "sr8") else F32,
                                        kind="ExternalOutput").ap()
    srow = sb.alloc("srow", [128, 4, 6], F32)
    g3c = sb.alloc("g3c", [128, 4, 2], F32)
    s3xb = sb.alloc("s3xb", [128, 128], F32)
    s3col = sb.alloc("s3col", [128, 128], BF16)
    sqs8 = sb.alloc("sqs8", [128, D // 2], F32)
    vop(lambda e: e.memset(s3col[:], 0.0), waits=wo_done)

    for g2 in range(2 if not no_ffn else 0):
        xqT = sb.alloc("xqT", [128, 4, 16, 128], BF16)
        xq_t = sb.alloc("xq_t", [128, D], BF16)
        h3b = sb.alloc("h3b", [128, CH], BF16)
        gv1 = sb.alloc("gv1", [128, CH], F32)
        gv2 = sb.alloc("gv2", [128, CH], F32)
        gv3 = sb.alloc("gv3", [128, CH], F32)
        gsave = sb.alloc("gsave", [128, 4, CH], BF16)
        tsl = sb.alloc("tsl", [128, 8, CH], BF16)
        bgb = sb.alloc("bgb", [128, CH], F32)
        bvb = sb.alloc("bvb", [128, CH], F32)
        f_x1g = flow(f"f_x1g_{g2}")
        f_slab = flow(f"f_slab_{g2}")
        f_tob = flow(f"f_tob_{g2}")

        dma("sync", x1g[:],
            X1s[g2 * 512:(g2 + 1) * 512, :].rearrange(
                "(a p) d -> p a d", p=128),
            f_x1g, waits=[x1_all] + (wo_done if g2 == 0 else fence()))
        x1lm = f_x1g.at()
        xq_free = None
        for i in range(4):
            vop(lambda e, i=i: e.tensor_reduce(
                srow[:, i, 0:1], x1g[:, i, :], axis=AX, op=OP.max,
                apply_absolute_value=True), waits=[x1lm])
        for i in range(4):
            vop(lambda e, i=i: e.tensor_scalar(
                srow[:, i, 0:1], srow[:, i, 0:1], 1e-5, None, OP.max))
        for i in range(4):
            vop(lambda e, i=i: e.reciprocal(srow[:, i, 1:2],
                                            srow[:, i, 0:1]))
        for i in range(4):
            vop(lambda e, i=i: e.tensor_scalar(
                srow[:, i, 1:2], srow[:, i, 1:2], 127.0, None, OP.mult))
        for i in range(4):
            vop(lambda e, i=i: e.tensor_mul(srow[:, i, 2:3],
                                            srow[:, i, 0:1], gam[:, 6:7]),
                waits=[gam_mark])
            vop(lambda e, i=i: e.tensor_mul(srow[:, i, 3:4],
                                            srow[:, i, 0:1], gam[:, 7:8]))
        for i in range(4):
            for hf in range(2):
                vop(lambda e, i=i, hf=hf: e.tensor_scalar(
                    sqs8[:], x1g[:, i, hf * 1024:(hf + 1) * 1024],
                    srow[:, i, 1:2], MAGIC, OP.mult, OP.add))
                vop(lambda e, hf=hf: e.tensor_scalar(
                    xq_t[:, hf * 1024:(hf + 1) * 1024], sqs8[:], MAGIC, None,
                    OP.subtract), waits=[xq_free] if hf == 0 else [])
            xqm = c_v.at()
            transpose_block(lambda a, i=i: xqT[:, i, a, :], xq_t, 16, [xqm])
            xq_free = c_t.at()
        xq_mark = c_v.at()
        if "xqt" in dbg and g2 == 0:
            dma("gpsimd", dbg_out["xqt"][:],
                xqT[:].rearrange("p i a t -> p (i a t)"), f_st,
                waits=[xq_mark, xq_free])

        slab_rd = None
        silu_prev = None
        h3b_free = None
        for hb in range(HID // CH):
            dma("sync", bgb[:], bg_bc[:, hb * CH:(hb + 1) * CH], f_slab,
                waits=[slab_rd] + quant_done_sp)
            dma("sync", bvb[:], bv_bc[:, hb * CH:(hb + 1) * CH], f_slab)
            for half, TS_ in ((0, TgS), (1, TgS), (2, TvS), (3, TvS)):
                hh = half % 2
                dma("sync", tsl[:],
                    TS_[hh * 1024:(hh + 1) * 1024,
                        hb * CH:(hb + 1) * CH].rearrange(
                        "(a p) n -> p a n", p=128), f_slab,
                    waits=[c_t.at()] +
                          ([x1_all] if (g2 == 0 and hb == 0 and half == 0)
                           else []))
                slm = f_slab.at()
                for i in range(4):
                    bk = pbank[i]
                    for a2 in range(8):
                        a = hh * 8 + a2
                        mw = ([slm, xq_mark, bank_rd.get(id(bk)),
                               av_evac.get(id(bk))] if a2 == 0 else [])
                        mm(bk[:], xqT[:, i, a, :], tsl[:, a2, :],
                           start=(a == 0), stop=(a == 15),
                           waits=mw,
                           inc=(c_t if (a == 15 or (i == 3 and a2 == 7))
                                else None))
                if half == 1:
                    for i in range(4):
                        bk = pbank[i]
                        vop(lambda e, bk=bk, i=i: e.scalar_tensor_tensor(
                            gv1[:], bk[:], srow[:, i, 2:3], bgb[:],
                            OP.mult, OP.add),
                            waits=[c_t.at(), silu_prev])
                        bank_rd[id(bk)] = c_v.at()
                        sop(lambda e: e.activation(gv2[:], gv1[:], AF.Silu),
                            waits=[c_v.at()])
                        silu_prev = c_s.at()
                        vop(lambda e, i=i: e.tensor_copy(gsave[:, i, :],
                                                         gv2[:]),
                            waits=[silu_prev])
            for i in range(4):
                bk = pbank[i]
                vop(lambda e, bk=bk, i=i: e.scalar_tensor_tensor(
                    gv3[:], bk[:], srow[:, i, 3:4], bvb[:],
                    OP.mult, OP.add), waits=[c_t.at(), h3b_free])
                bank_rd[id(bk)] = c_v.at()
                vop(lambda e, i=i: e.tensor_mul(h3b[:], gsave[:, i, :],
                                                gv3[:]))
                vop(lambda e, i=i: e.tensor_reduce(
                    g3c[:, i, 1:2], h3b[:], axis=AX, op=OP.max,
                    apply_absolute_value=True))
                h3m = c_v.at()
                transpose_block(
                    lambda s2, i=i, hb=hb: h3T[:, i, hb * 4 + s2, :],
                    h3b, 4, [h3m] + (wo_done if (g2 == 0 and hb == 0)
                                     else []))
                h3b_free = c_t.at()
                if hb == 0:
                    vop(lambda e, i=i: e.tensor_copy(g3c[:, i, 0:1],
                                                     g3c[:, i, 1:2]))
                else:
                    vop(lambda e, i=i: e.tensor_max(
                        g3c[:, i, 0:1], g3c[:, i, 0:1], g3c[:, i, 1:2]))
            slab_rd = c_t.at()
        h3_mark = c_v.at()
        h3tap = []
        if "h3r" in dbg and g2 == 0:
            dma("gpsimd", dbg_out["h3r"][:],
                arena_b[:, 0:32768], f_st, waits=[h3_mark])
            h3tap = [f_st.at()]
        sb.free("bvb", "bgb", "tsl", "gsave", "gv3", "gv2", "gv1", "h3b",
                "xq_t", "xqT")

        qtmp = sb.alloc("qtmp", [128, 2, 128], F32)
        for i in range(4):
            vop(lambda e, i=i: e.tensor_scalar(
                g3c[:, i, 0:1], g3c[:, i, 0:1], 1e-5, None, OP.max),
                waits=[h3_mark])
        for i in range(4):
            vop(lambda e, i=i: e.reciprocal(g3c[:, i, 1:2], g3c[:, i, 0:1]))
            vop(lambda e, i=i: e.tensor_mul(srow[:, i, 2:3], g3c[:, i, 0:1],
                                            gam[:, 8:9]))
        for i in range(4):
            vop(lambda e, i=i: e.tensor_scalar(
                g3c[:, i, 1:2], g3c[:, i, 1:2], 127.0, None, OP.mult))
        for i in range(4):
            tb = psT[0]
            vop(lambda e, i=i: e.tensor_copy(s3col[:, 0:1], g3c[:, i, 1:2]))
            vop(lambda e: e.tensor_copy(epsc[:], epsc[:]))
            transp(tb[:, 0:128], s3col[:], waits=[c_v.at(), psT_rd[0]],
                   inc=c_t)
            vop(lambda e, tb=tb: e.tensor_copy(s3row[0:1, :],
                                               tb[0:1, 0:128]),
                waits=[c_t.at()])
            psT_rd[0] = c_v.at()
            mm(pbank[0][:, 0:128], onesrow_bf[:], s3row[0:1, :],
               start=True, stop=True,
               waits=[c_v.at(), bank_rd.get(id(pbank[0])),
                      av_evac.get(id(pbank[0]))], inc=c_t)
            vop(lambda e: e.tensor_copy(s3xb[:], pbank[0][:, 0:128]),
                waits=[c_t.at()])
            bank_rd[id(pbank[0])] = c_v.at()
            vop(lambda e: e.tensor_copy(lnc[:, 6:7], lnc[:, 6:7]))
            s3bm = c_v.at()
            for a in range(64):
                vop(lambda e, i=i, a=a: e.tensor_mul(
                    qtmp[:, a % 2, :], h3T[:, i, a, :], s3xb[:]),
                    waits=([s3bm] + h3tap) if a == 0 else [])
                if a > 0:
                    vop(lambda e, i=i, a=a: e.tensor_scalar(
                        h3T[:, i, a - 1, :], qtmp[:, (a - 1) % 2, :],
                        MAGIC, MAGIC, OP.add, OP.subtract))
            vop(lambda e: e.tensor_copy(epsc[:], epsc[:]))
            vop(lambda e, i=i: e.tensor_scalar(
                h3T[:, i, 63, :], qtmp[:, 1, :], MAGIC, MAGIC, OP.add,
                OP.subtract))
        h3q_mark = c_v.at()
        h3qtap = []
        if "h3q" in dbg and g2 == 0:
            dma("gpsimd", dbg_out["h3q"][:],
                arena_b[:, 0:32768], f_st, waits=[h3q_mark])
            h3qtap = [f_st.at()]
        if "sr8" in dbg and g2 == 0:
            dma("gpsimd", dbg_out["sr8"][:, 0:24],
                srow[:].rearrange("p i c -> p (i c)"), f_st,
                waits=[h3q_mark])
            h3qtap = h3qtap + [f_st.at()]
        sb.free("qtmp")

        DBW = 256
        tob = sb.alloc("tob", [128, 64, DBW], BF16)
        bob = sb.alloc("bob", [128, DBW], F32)
        to_rd = None
        for db in range(D // DBW):
            dma("sync", tob[:],
                ToS[:, db * DBW:(db + 1) * DBW].rearrange(
                    "(a p) n -> p a n", p=128),
                f_tob, waits=[to_rd] + quant_done_sp +
                             ([h3q_mark, c_t.at()] if db == 0 else []))
            dma("sync", bob[:], bout_bc[:, db * DBW:(db + 1) * DBW], f_tob)
            tolm = f_tob.at()
            for i in range(4):
                bk = pbank[i % 4]
                for a in range(64):
                    mw = ([tolm, h3q_mark, bank_rd.get(id(bk)),
                           av_evac.get(id(bk))] + h3qtap) if a == 0 else []
                    mm(bk[:, 0:DBW], h3T[:, i, a, :], tob[:, a, :],
                       start=(a == 0), stop=(a == 63),
                       waits=mw, inc=(c_t if a == 63 else None))
                tm = c_t.at()
                vop(lambda e, bk=bk, i=i, db=db: e.scalar_tensor_tensor(
                    r2g[:, i, db * DBW:(db + 1) * DBW], bk[:, 0:DBW],
                    srow[:, i, 2:3], bob[:], OP.mult, OP.add),
                    waits=[tm])
                bank_rd[id(bk)] = c_v.at()
            to_rd = c_t.at()
        r2tap = []
        if "r2" in dbg and g2 == 0:
            r2m = c_v.at()
            dma("gpsimd", dbg_out["r2"][:],
                r2g[:].rearrange("p i d -> p (i d)"), f_st, waits=[r2m])
            r2tap = [f_st.at()]
        for i in range(4):
            t = g2 * 4 + i
            vop(lambda e, i=i: e.tensor_add(r2g[:, i, :], r2g[:, i, :],
                                            x1g[:, i, :]),
                waits=[x1lm] + r2tap)
            om = layernorm(r2g[:, i, :], c_v.at(), scratch=sqs8)
            dma("sync", OUT[t * 128:(t + 1) * 128, :], r2g[:, i, :], f_st,
                waits=[om])
        sb.free("bob", "tob")

    final_waits = [fl.at() for fl in flows.values()]
    rec.emit("sync", lambda e: e.nop(), waits=final_waits)
    rec.emit("sync", lambda e: e.sem_clear(range(SEM_LO, SEM_HI)))

    with nc.Block() as block:
        rec.replay(block)
    ctx.close()
    return nc



def _rope_tables():
    idx = np.arange(DH // 2, dtype=np.float32)
    pos = np.arange(S, dtype=np.float32)
    freq = np.float32(10000.0) ** (np.float32(-2.0) * idx / np.float32(DH))
    theta = pos[:, None].astype(np.float32) * freq[None, :].astype(np.float32)
    cos = np.cos(theta.astype(np.float64)).astype(np.float32)
    sin = np.sin(theta.astype(np.float64)).astype(np.float32)
    cosT = np.ascontiguousarray(np.tile(cos, (1, H))).astype(BF)
    sinT = np.ascontiguousarray(np.tile(sin, (1, H))).astype(BF)
    return cosT, sinT


def kernel(
    L0_attn_h, L0_attn_W_DKV_w, L0_attn_W_DKV_b, L0_attn_W_UK_w,
    L0_attn_W_UK_b, L0_attn_W_UV_w, L0_attn_W_UV_b, L0_attn_W_DQ_w,
    L0_attn_W_DQ_b, L0_attn_W_UQ_w, L0_attn_W_UQ_b, L0_attn_W_KR_w,
    L0_attn_W_KR_b, L0_attn_W_O_w, L0_attn_W_O_b, L0_ffn_gate_w,
    L0_ffn_gate_b, L0_ffn_val_w, L0_ffn_val_b, L0_ffn_out_w, L0_ffn_out_b,
    _debug=(), _cores=None, _no_ffn=False, _ablate=(),
):
    f32 = np.float32
    h = np.asarray(L0_attn_h, f32)

    def T(w):
        return np.ascontiguousarray(np.asarray(w, f32).T)

    def bc(b):
        bb = np.asarray(b, f32)
        return np.ascontiguousarray(
            np.broadcast_to(bb[None, :], (128, bb.shape[0])))

    def col(b, n):
        return np.ascontiguousarray(np.asarray(b, f32).reshape(n, 128).T)

    shared = {
        "wdkvT": T(L0_attn_W_DKV_w),
        "bdkv_col": col(L0_attn_W_DKV_b, DC // 128),
        "wdqT": T(L0_attn_W_DQ_w), "bdq_col": col(L0_attn_W_DQ_b, DQC // 128),
        "wuqT": T(L0_attn_W_UQ_w), "buq_bc": bc(L0_attn_W_UQ_b),
        "wukT": T(L0_attn_W_UK_w), "buk_bc": bc(L0_attn_W_UK_b),
        "wuvT": T(L0_attn_W_UV_w), "buv_bc": bc(L0_attn_W_UV_b),
        "wkrT": T(L0_attn_W_KR_w), "bkr_bc": bc(L0_attn_W_KR_b),
        "woT": T(L0_attn_W_O_w), "bo_bc": bc(L0_attn_W_O_b),
        "wgT": T(L0_ffn_gate_w), "bg_bc": bc(L0_ffn_gate_b),
        "wvT": T(L0_ffn_val_w), "bv_bc": bc(L0_ffn_val_b),
        "woutT": T(L0_ffn_out_w), "bout_bc": bc(L0_ffn_out_b),
        "identI": np.eye(128, dtype=BF),
    }
    cosT, sinT = _rope_tables()
    shared["cosT"] = cosT
    shared["sinT"] = sinT

    hTb = [np.ascontiguousarray(h[b].T) for b in range(B)]

    in_maps = []
    ncores = 8 if _cores is None else _cores
    for core in range(ncores):
        b, j = core // 4, core % 4
        own = np.concatenate(
            [np.arange(128 * (j + 4 * t), 128 * (j + 4 * t) + 128)
             for t in range(NT)])
        mask = (np.arange(CH)[None, :] <= 128 * j + np.arange(128)[:, None])
        m = dict(shared)
        m["hT"] = hTb[b]
        m["hTown"] = np.ascontiguousarray(hTb[b][:, own])
        m["h_own"] = np.ascontiguousarray(h[b][own])
        m["cos_own"] = np.ascontiguousarray(cosT[own])
        m["sin_own"] = np.ascontiguousarray(sinT[own])
        m["maskI"] = mask.astype(BF)
        in_maps.append(m)

    nc = build_program(debug=_debug, no_ffn=_no_ffn, ablate=_ablate)
    res = run_bass_kernel_spmd(nc, in_maps, core_ids=list(range(ncores)))

    out = np.zeros((B, S, D), np.float32)
    for core in range(ncores):
        b, j = core // 4, core % 4
        co = res.results[core]["out"]
        for t in range(NT):
            out[b, 128 * (j + 4 * t):128 * (j + 4 * t) + 128] = \
                co[128 * t:128 * t + 128]
    if _debug:
        return out, res
    return out


# version 02
# speedup vs baseline: 1.1036x; 1.1036x over previous
import numpy as np
import ml_dtypes
from contextlib import ExitStack
import concourse.bass as bass
import concourse.mybir as mybir
from concourse.bass_utils import run_bass_kernel_spmd

F32 = mybir.dt.float32
BF16 = mybir.dt.bfloat16
AX = mybir.AxisListType.X
OP = mybir.AluOpType
AF = mybir.ActivationFunctionType

B, S, D = 2, 4096, 2048
H, DH = 16, 128
DC, DQC = 512, 1024
HID = 8192
NT = 8
NTOK = NT * 128
ST = S // 128
CH = 512
NCH = S // CH
MAGIC = 3 * 2.0 ** 22
SM_SCALE = 1.0 / 16.0
LN_EPS = 1e-5
BF = ml_dtypes.bfloat16

ENGS = ("tensor", "vector", "scalar", "gpsimd", "sync")


class Flow:
    __slots__ = ("sem", "count", "name")

    def __init__(self, sem, name=""):
        self.sem = sem
        self.count = 0
        self.name = name

    def at(self):
        return (self, self.count)


class Rec:
    def __init__(self, nc):
        self.nc = nc
        self.lists = {k: [] for k in ENGS}
        self.last_wait = {}

    def emit(self, eng, fn, waits=(), inc=None, inc_n=1):
        wl = []
        for w in waits:
            if w is None:
                continue
            fl, val = w
            if val <= 0:
                continue
            key = (eng, id(fl))
            if self.last_wait.get(key, -1) >= val:
                continue
            self.last_wait[key] = val
            wl.append((fl.sem, val))
        sem = None
        if inc is not None:
            inc.count += inc_n
            sem = inc.sem

        def thunk(e, wl=wl, fn=fn, sem=sem, n=inc_n):
            for s, v in wl:
                e.wait_ge(s, v)
            instr = fn(e)
            if sem is not None:
                instr.then_inc(sem, n)

        self.lists[eng].append(thunk)

    def replay(self, block):
        for eng in ENGS:
            lst = self.lists[eng]
            getattr(block, eng)(lambda e, lst=lst: [f(e) for f in lst])


class SBAlloc:
    def __init__(self, nc):
        self.nc = nc
        self.stack = []
        self.serial = 0

    def alloc(self, name, shape, dt):
        self.serial += 1
        g = self.nc.sbuf_tensor(f"{name}__{self.serial}", list(shape), dt)
        h = g.__enter__()
        self.stack.append((name, g))
        return h

    def free(self, *names):
        want = set(names)
        for _ in range(len(want)):
            n, g = self.stack.pop()
            assert n in want, f"free order violation: {n} vs {want}"
            want.discard(n)
            g.__exit__(None, None, None)


def build_program(debug=(), no_ffn=False, ablate=()):
    nc = bass.Bass("TRN2", target_bir_lowering=False)
    nc.sbuf_top = 199680
    dbg = set(debug)

    def din(name, shape, dt=F32):
        return nc.dram_tensor(name, list(shape), dt, kind="ExternalInput").ap()

    hT = din("hT", [D, S])
    hTown = din("hTown", [D, NTOK])
    h_own = din("h_own", [NTOK, D])
    wdkvT = din("wdkvT", [D, DC])
    bdkv_col = din("bdkv_col", [128, DC // 128])
    wdqT = din("wdqT", [D, DQC])
    bdq_col = din("bdq_col", [128, DQC // 128])
    wuqT = din("wuqT", [DQC, D])
    buq_bc = din("buq_bc", [128, D])
    wukT = din("wukT", [DC, D])
    buk_bc = din("buk_bc", [128, D])
    wuvT = din("wuvT", [DC, D])
    buv_bc = din("buv_bc", [128, D])
    wkrT = din("wkrT", [D, D])
    bkr_bc = din("bkr_bc", [128, D])
    woT = din("woT", [D, D])
    bo_bc = din("bo_bc", [128, D])
    wgT = din("wgT", [D, HID])
    bg_bc = din("bg_bc", [128, HID])
    wvT = din("wvT", [D, HID])
    bv_bc = din("bv_bc", [128, HID])
    woutT = din("woutT", [HID, D])
    bout_bc = din("bout_bc", [128, D])
    cosT = din("cosT", [S, 1024], BF16)
    sinT = din("sinT", [S, 1024], BF16)
    cos_own = din("cos_own", [NTOK, 1024], BF16)
    sin_own = din("sin_own", [NTOK, 1024], BF16)
    identI = din("identI", [128, 128], BF16)
    maskI = din("maskI", [128, CH], BF16)

    OUT = nc.dram_tensor("out", [NTOK, D], F32, kind="ExternalOutput").ap()

    def scrt(name, shape, dt=BF16):
        return nc.dram_tensor(name, list(shape), dt, kind="Internal").ap()

    KCs = scrt("KCs", [S, D])
    KRs = scrt("KRs", [S, D])
    Vs = scrt("Vs", [S, D])
    X1s = scrt("X1s", [NTOK, D], F32)
    TgS = scrt("TgS", [D, HID])
    TvS = scrt("TvS", [D, HID])
    ToS = scrt("ToS", [HID, D])

    dbg_out = {}
    if dbg:
        shapes = {
            "ckv": [DC, S], "cq": [DQC, NTOK], "qc": [NTOK, D],
            "qr": [NTOK, D], "kr": [S, D], "kc": [S, D], "v": [S, D],
            "gam": [128, 12], "o": [NTOK, D], "x1": [NTOK, D],
            "tg": [D, HID], "lacc": [128, NT * H], "oacc": [NTOK, D],
            "r1": [NTOK, D], "ot": [NTOK, D], "x1s": [NTOK, D],
            "lnx": [128, NT * 8], "wod": [128, 32768],
        }
        for k in dbg:
            if k in shapes:
                dbg_out[k] = nc.dram_tensor("dbg_" + k, shapes[k], F32,
                                            kind="ExternalOutput").ap()

    rec = Rec(nc)
    sb = SBAlloc(nc)
    ctx = ExitStack()

    flows = {}

    def flow(name):
        if name not in flows:
            sem = ctx.enter_context(nc.semaphore(name))
            flows[name] = Flow(sem, name)
        return flows[name]

    boot = flow("boot")
    f_ld = flow("f_ld")
    f_g = flow("f_g")
    f_sp = flow("f_sp")
    f_st = flow("f_st")
    c_t = flow("c_t")
    c_v = flow("c_v")
    c_s = flow("c_s")

    pbank = [ctx.enter_context(nc.psum_tensor(f"pb{i}", [128, 512], F32))
             for i in range(4)]
    psT = [ctx.enter_context(nc.psum_tensor(f"psT{i}", [128, 1024], BF16))
           for i in range(2)]

    def dma(queue, out, in_, fl, waits=()):
        rec.emit(queue, lambda e, o=out, i=in_: e.dma_start(out=o, in_=i),
                 waits=waits, inc=fl, inc_n=16)

    def dmaT(out, in_, fl, waits=()):
        rec.emit("sync",
                 lambda e, o=out, i=in_: e.dma_start(out=o, in_=i,
                                                     transpose=True),
                 waits=waits, inc=fl, inc_n=16)

    def mm(out, lhsT, rhs, start, stop, waits=(), inc=None):
        rec.emit("tensor",
                 lambda e, o=out, l=lhsT, r=rhs, a=start, b=stop:
                 e.matmul(o, l, r, start=a, stop=b),
                 waits=waits, inc=inc)

    def transp(out, in_, waits=(), inc=None):
        rec.emit("tensor",
                 lambda e, o=out, i=in_: e.transpose(o, i, ident_sb[:]),
                 waits=waits, inc=inc)

    def vop(fn, waits=()):
        rec.emit("vector", fn, waits=waits, inc=c_v)

    def sop(fn, waits=()):
        rec.emit("scalar", fn, waits=waits, inc=c_s)

    def fence():
        return [c_t.at(), c_v.at(), c_s.at()]

    bank_rd = {}

    SEM_LO, SEM_HI = 155, 256
    for _eng in ENGS:
        rec.emit(_eng, lambda e: e.sem_clear(range(SEM_LO, SEM_HI)))

    ident_sb = sb.alloc("ident_sb", [128, 128], BF16)
    mask_sb = sb.alloc("mask_sb", [128, CH], BF16)
    epsc = sb.alloc("epsc", [128, 1], F32)
    onescol = sb.alloc("onescol", [128, 1], F32)
    onesrow = sb.alloc("onesrow", [1, 128], F32)
    onesrow_bf = sb.alloc("onesrow_bf", [1, 128], BF16)
    s3row = sb.alloc("s3row", [1, 128], BF16)
    gam = sb.alloc("gam", [128, 12], F32)
    lnc = sb.alloc("lnc", [128, 8], F32)

    f_const = flow("f_const")
    dma("sync", ident_sb[:], identI[:], f_const)
    dma("sync", mask_sb[:], maskI[:], f_const)
    vop(lambda e: e.memset(epsc[:], LN_EPS))
    vop(lambda e: e.memset(onescol[:], 1.0))
    vop(lambda e: e.memset(onesrow[:], 1.0))
    vop(lambda e: e.memset(onesrow_bf[:], 1.0))
    init_mark = c_v.at()
    ld0 = f_const.at()

    arena_b = sb.alloc("arena_b", [128, 32768], BF16)
    qCT = arena_b[:, 0:16384].rearrange("p (h t) -> p h t", h=H)
    qRT = arena_b[:, 16384:32768].rearrange("p (h t) -> p h t", h=H)
    wo_view = arena_b[:, 0:32768].rearrange("p (a n) -> p a n", a=16)
    h3T = arena_b[:, 0:32768].rearrange("p (i a t) -> p i a t", i=4, a=64)
    ckvT = arena_b[:, 0:16384].rearrange("p (a t) -> p a t", a=DC // 128)

    WMATS = [(wgT, D, HID, TgS, 0), (wvT, D, HID, TvS, 1),
             (woutT, HID, D, ToS, 2)]
    gbuf = [sb.alloc(f"gbuf{i}", [128, 2048], F32) for i in range(2)]
    f_gb = [flow("f_gb0"), flow("f_gb1")]
    gacc = sb.alloc("gacc", [128, 4], F32)
    gtmp = sb.alloc("gtmp", [128, 2], F32)

    for wT, nr, ncol, _, m in WMATS:
        vop(lambda e, m=m: e.memset(gacc[:, m:m + 1], 0.0),
            waits=[c_t.at()])
        slot_rd = [None, None]
        idx = 0
        nch = (nr // 128) * (ncol // 2048)
        for a in range(nr // 128):
            for cb in range(ncol // 2048):
                sl = idx % 2
                dma("sync", gbuf[sl][:],
                    wT[a * 128:(a + 1) * 128, cb * 2048:(cb + 1) * 2048],
                    f_gb[sl], waits=[slot_rd[sl]])
                lm = f_gb[sl].at()
                vop(lambda e, sl=sl: e.reduce_sum(
                    gtmp[:, sl:sl + 1], gbuf[sl][:], axis=AX,
                    apply_absolute_value=True), waits=[lm])
                if idx > 0:
                    psl = (idx - 1) % 2
                    vop(lambda e, psl=psl, m=m: e.tensor_add(
                        gacc[:, m:m + 1], gacc[:, m:m + 1],
                        gtmp[:, psl:psl + 1]))
                slot_rd[sl] = c_v.at()
                idx += 1
        vop(lambda e: e.tensor_copy(gtmp[:, 0:1], gtmp[:, 0:1]))
        vop(lambda e, m=m: e.tensor_add(
            gacc[:, m:m + 1], gacc[:, m:m + 1],
            gtmp[:, (nch - 1) % 2:(nch - 1) % 2 + 1]))
        mm(pbank[3][0:1, m:m + 1], gacc[:, m:m + 1], onescol[:],
           start=True, stop=True,
           waits=[c_v.at(), init_mark, bank_rd.get(id(pbank[3]))], inc=c_t)
        vop(lambda e, m=m, n=float(nr * ncol): e.tensor_scalar(
            gam[0:1, 9 + m:10 + m], pbank[3][0:1, m:m + 1],
            1.0 / n, 1e-5, OP.mult, OP.max), waits=[c_t.at()])
        bank_rd[id(pbank[3])] = c_v.at()
        mm(pbank[3][:, 256 + m:257 + m], onesrow[:], gam[0:1, 9 + m:10 + m],
           start=True, stop=True, waits=[c_v.at()], inc=c_t)
        vop(lambda e, m=m: e.tensor_copy(gam[:, m:m + 1],
                                         pbank[3][:, 256 + m:257 + m]),
            waits=[c_t.at()])
        bank_rd[id(pbank[3])] = c_v.at()
        vop(lambda e: e.tensor_copy(lnc[:, 6:7], lnc[:, 6:7]))
        vop(lambda e, m=m: e.reciprocal(gam[:, 3 + m:4 + m], gam[:, m:m + 1]))
        vop(lambda e, m=m: e.tensor_scalar(gam[:, 6 + m:7 + m],
                                           gam[:, m:m + 1],
                                           1.0 / 127.0, None, OP.mult))
    gam_mark = c_v.at()
    if "gam" in dbg:
        vop(lambda e: e.tensor_copy(gam[:, 9:12], gacc[:, 0:3]),
            waits=[gam_mark, c_t.at()])
        dma("sync", dbg_out["gam"][:], gam[:], f_st, waits=[c_v.at()])

    qout = [sb.alloc(f"qout{i}", [128, 2048], BF16) for i in range(2)]
    f_tw = [flow("f_tw0"), flow("f_tw1")]
    for wT, nr, ncol, Tdst, m in WMATS:
        slot_rd = [None, None]
        out_wr = [None, None]
        idx = 0
        for a in range(nr // 128):
            for cb in range(ncol // 2048):
                sl = idx % 2
                dma("sync", gbuf[sl][:],
                    wT[a * 128:(a + 1) * 128, cb * 2048:(cb + 1) * 2048],
                    f_gb[sl], waits=[slot_rd[sl], gam_mark])
                lm = f_gb[sl].at()
                vop(lambda e, sl=sl, m=m: e.tensor_scalar(
                    gbuf[sl][:], gbuf[sl][:], gam[:, 3 + m:4 + m], MAGIC,
                    OP.mult, OP.add), waits=[lm, out_wr[sl]])
                vop(lambda e, sl=sl: e.tensor_scalar(
                    gbuf[sl][:], gbuf[sl][:], MAGIC + 1.0, MAGIC - 1.0,
                    OP.min, OP.max))
                vop(lambda e, sl=sl: e.tensor_scalar(
                    qout[sl][:], gbuf[sl][:], MAGIC, None, OP.subtract))
                slot_rd[sl] = c_v.at()
                dma("sync",
                    Tdst[a * 128:(a + 1) * 128, cb * 2048:(cb + 1) * 2048],
                    qout[sl][:], f_tw[sl], waits=[slot_rd[sl]])
                out_wr[sl] = f_tw[sl].at()
                idx += 1
    quant_done_sp = [f_tw[0].at(), f_tw[1].at()]
    if "tg" in dbg:
        dma("gpsimd", dbg_out["tg"][:], TgS[:], f_st, waits=quant_done_sp)
    sb.free("qout1", "qout0", "gtmp", "gacc", "gbuf1", "gbuf0")


    def proj_feature_major(wT_d, ndcb, bcol_d, src_d, ntot, dst, dbg_key):
        CCH = 256
        w_sb = sb.alloc("pf_w", [128, 16, 128 * ndcb], BF16)
        b_sb = sb.alloc("pf_b", [128, ndcb], F32)
        htb = [sb.alloc(f"pf_ht{i}", [128, 16, CCH], BF16) for i in range(2)]
        f_pfw = flow(f"f_pfw_{dbg_key}")
        f_pfb = flow(f"f_pfb_{dbg_key}")
        f_pfh = [flow(f"f_pfh0_{dbg_key}"), flow(f"f_pfh1_{dbg_key}")]
        dma("gpsimd", w_sb[:], wT_d[:].rearrange("(a p) n -> p a n", p=128),
            f_pfw, waits=fence())
        dma("sync", b_sb[:], bcol_d[:], f_pfb, waits=fence())
        wm = f_pfw.at()
        bm = f_pfb.at()
        slot_rd = [None, None]
        for cc in range(ntot // CCH):
            sl = cc % 2
            dma("gpsimd", htb[sl][:],
                src_d[:, cc * CCH:(cc + 1) * CCH].rearrange(
                    "(a p) t -> p a t", p=128), f_pfh[sl],
                waits=[slot_rd[sl]])
            lm = f_pfh[sl].at()
            for dcb in range(ndcb):
                bk = pbank[dcb % 2]
                for a in range(16):
                    mw = [lm, wm, bank_rd.get(id(bk))] if a == 0 else []
                    mm(bk[:, 0:CCH], w_sb[:, a, dcb * 128:(dcb + 1) * 128],
                       htb[sl][:, a, :], start=(a == 0), stop=(a == 15),
                       waits=mw, inc=(c_t if a == 15 else None))
                tm = c_t.at()
                vop(lambda e, bk=bk, dcb=dcb, cc=cc: e.tensor_scalar(
                    dst[:, dcb, cc * CCH:(cc + 1) * CCH], bk[:, 0:CCH],
                    b_sb[:, dcb:dcb + 1], None, OP.add), waits=[tm, bm])
                bank_rd[id(bk)] = c_v.at()
            slot_rd[sl] = c_t.at()
        if dbg_key in dbg:
            for dcb in range(ndcb):
                dma("gpsimd", dbg_out[dbg_key][dcb * 128:(dcb + 1) * 128, :],
                    dst[:, dcb, :], f_st, waits=[c_v.at()])
        sb.free("pf_ht1", "pf_ht0", "pf_b", "pf_w")

    proj_feature_major(wdkvT, DC // 128, bdkv_col, hT, S, ckvT, "ckv")

    def proj_token_major_kv(wT_d, b_bc_d, DSTs, dbg_key):
        w_sb = sb.alloc("kv_w", [128, DC // 128, D], BF16)
        b_sb = sb.alloc("kv_b", [128, D], F32)
        out_t = sb.alloc("kv_t", [128, D], BF16)
        f_kvw = flow(f"f_kvw_{dbg_key}")
        f_kvb = flow(f"f_kvb_{dbg_key}")
        dma("gpsimd", w_sb[:], wT_d[:].rearrange("(a p) n -> p a n", p=128),
            f_kvw, waits=fence())
        dma("sync", b_sb[:], b_bc_d[:], f_kvb, waits=fence())
        wm = f_kvw.at()
        bm = f_kvb.at()
        spill_last = None
        for st in range(ST):
            for hb in range(4):
                bk = pbank[hb % 4]
                for dcb in range(DC // 128):
                    mw = [wm, bank_rd.get(id(bk))] if dcb == 0 else []
                    mm(bk[:], ckvT[:, dcb, st * 128:(st + 1) * 128],
                       w_sb[:, dcb, hb * 512:(hb + 1) * 512],
                       start=(dcb == 0), stop=(dcb == DC // 128 - 1),
                       waits=mw, inc=(c_t if dcb == DC // 128 - 1 else None))
                tm = c_t.at()
                vop(lambda e, bk=bk, hb=hb: e.tensor_tensor(
                    out_t[:, hb * 512:(hb + 1) * 512], bk[:],
                    b_sb[:, hb * 512:(hb + 1) * 512], OP.add),
                    waits=[tm, bm, spill_last])
                bank_rd[id(bk)] = c_v.at()
            dma("sync", DSTs[st * 128:(st + 1) * 128, :], out_t[:], f_sp,
                waits=[c_v.at()])
            spill_last = f_sp.at()
        if dbg_key in dbg:
            dma("gpsimd", dbg_out[dbg_key][:], DSTs[:], f_st,
                waits=[f_sp.at()])
        sb.free("kv_t", "kv_b", "kv_w")
        return f_sp.at()

    proj_token_major_kv(wukT, buk_bc, KCs, "kc")
    kv_spill_mark = proj_token_major_kv(wuvT, buv_bc, Vs, "v")

    cqT = sb.alloc("cqT", [128, DQC // 128, NTOK], BF16)
    proj_feature_major(wdqT, DQC // 128, bdq_col, hTown, NTOK, cqT, "cq")

    def rope_tile(dst, src, cbuf, sbuf_, tmp, ncols, waits):
        n2 = 2 * ncols
        vop(lambda e: e.tensor_mul(dst[:, 0:n2:2], src[:, 0:n2:2], cbuf[:]),
            waits=waits)
        vop(lambda e: e.tensor_mul(tmp[:], src[:, 1:n2:2], sbuf_[:]))
        vop(lambda e: e.tensor_sub(dst[:, 0:n2:2], dst[:, 0:n2:2], tmp[:]))
        vop(lambda e: e.tensor_mul(dst[:, 1:n2:2], src[:, 0:n2:2], sbuf_[:]))
        vop(lambda e: e.tensor_mul(tmp[:], src[:, 1:n2:2], cbuf[:]))
        vop(lambda e: e.tensor_add(dst[:, 1:n2:2], dst[:, 1:n2:2], tmp[:]))

    psT_rd = [None, None]

    def transpose_block(dst_fn, src, nchunks, waits):
        for a in range(nchunks):
            tb = psT[a % 2]
            transp(tb[:, 0:128], src[:, a * 128:(a + 1) * 128],
                   waits=list(waits) + [psT_rd[a % 2]], inc=c_t)
            tm = c_t.at()
            vop(lambda e, tb=tb, a=a: e.tensor_copy(dst_fn(a), tb[:, 0:128]),
                waits=[tm])
            psT_rd[a % 2] = c_v.at()
        return c_v.at()

    wuq_sb = sb.alloc("wuq_sb", [128, DQC // 128, D], BF16)
    buq_sb = sb.alloc("buq_sb", [128, D], F32)
    qc_t = sb.alloc("qc_t", [128, D], BF16)
    qr_t = sb.alloc("qr_t", [128, D], BF16)
    rtmp = sb.alloc("rtmp", [128, 1024], BF16)
    csb = [sb.alloc(f"csb{i}", [128, 1024], BF16) for i in range(2)]
    ssb = [sb.alloc(f"ssb{i}", [128, 1024], BF16) for i in range(2)]
    f_q3w = flow("f_q3w")
    f_q3b = flow("f_q3b")
    f_cs = [flow("f_cs0"), flow("f_cs1")]
    dma("gpsimd", wuq_sb[:], wuqT[:].rearrange("(a p) n -> p a n", p=128),
        f_q3w, waits=fence())
    dma("sync", buq_sb[:], buq_bc[:], f_q3b, waits=fence())
    wm = f_q3w.at()
    bm = f_q3b.at()
    cs_rd = [None, None]
    qtr_prev = None
    qdbg_prev = None
    for ot in range(NT):
        sl = ot % 2
        dma("sync", csb[sl][:], cos_own[ot * 128:(ot + 1) * 128, :],
            f_cs[sl], waits=[cs_rd[sl]])
        dma("sync", ssb[sl][:], sin_own[ot * 128:(ot + 1) * 128, :],
            f_cs[sl])
        csm = f_cs[sl].at()
        for hb in range(4):
            bk = pbank[hb % 2]
            for qcb in range(DQC // 128):
                mw = [wm, bank_rd.get(id(bk))] if qcb == 0 else []
                mm(bk[:], cqT[:, qcb, ot * 128:(ot + 1) * 128],
                   wuq_sb[:, qcb, hb * 512:(hb + 1) * 512],
                   start=(qcb == 0), stop=(qcb == DQC // 128 - 1),
                   waits=mw, inc=(c_t if qcb == DQC // 128 - 1 else None))
            tm = c_t.at()
            vop(lambda e, bk=bk, hb=hb: e.tensor_tensor(
                qc_t[:, hb * 512:(hb + 1) * 512], bk[:],
                buq_sb[:, hb * 512:(hb + 1) * 512], OP.add),
                waits=[tm, bm, qtr_prev, qdbg_prev])
            bank_rd[id(bk)] = c_v.at()
        qc_mark = c_v.at()
        rope_tile(qr_t, qc_t, csb[sl], ssb[sl], rtmp, 1024,
                  [qc_mark, csm, qtr_prev, qdbg_prev])
        rope_mark = c_v.at()
        cs_rd[sl] = rope_mark
        transpose_block(lambda a, ot=ot: qCT[:, a, ot * 128:(ot + 1) * 128],
                        qc_t, 16, [qc_mark, ld0])
        transpose_block(lambda a, ot=ot: qRT[:, a, ot * 128:(ot + 1) * 128],
                        qr_t, 16, [rope_mark])
        qtr_prev = c_t.at()
        if "qc" in dbg:
            dma("gpsimd", dbg_out["qc"][ot * 128:(ot + 1) * 128, :], qc_t[:],
                f_st, waits=[qc_mark])
        if "qr" in dbg:
            dma("gpsimd", dbg_out["qr"][ot * 128:(ot + 1) * 128, :], qr_t[:],
                f_st, waits=[rope_mark])
        if "qc" in dbg or "qr" in dbg:
            qdbg_prev = f_st.at()
    sb.free("ssb1", "ssb0", "csb1", "csb0", "rtmp", "qr_t", "qc_t",
            "buq_sb", "wuq_sb", "cqT")

    wkr_sb = sb.alloc("wkr_sb", [128, 16, 1024], BF16)
    bkr_sb = sb.alloc("bkr_sb", [128, D], F32)
    kr_t = sb.alloc("kr_t", [128, 1024], BF16)
    krr_t = sb.alloc("krr_t", [128, 1024], BF16)
    rtmp = sb.alloc("rtmp", [128, 512], BF16)
    csb = [sb.alloc(f"csb{i}", [128, 512], BF16) for i in range(2)]
    ssb = [sb.alloc(f"ssb{i}", [128, 512], BF16) for i in range(2)]
    htb = [sb.alloc(f"htb{i}", [128, 16, 128], BF16) for i in range(2)]
    f_p4w = flow("f_p4w")
    f_p4b = flow("f_p4b")
    f_p4h = [flow("f_p4h0"), flow("f_p4h1")]
    f_p4c = [flow("f_p4c0"), flow("f_p4c1")]
    dma("sync", bkr_sb[:], bkr_bc[:], f_p4b, waits=fence())
    bm = f_p4b.at()
    for half in range(2):
        dma("gpsimd", wkr_sb[:],
            wkrT[:, half * 1024:(half + 1) * 1024].rearrange(
                "(a p) n -> p a n", p=128), f_p4w, waits=fence())
        wm = f_p4w.at()
        slot_rd = [None, None]
        cs_rd = [None, None]
        spill_last = None
        for st in range(ST):
            sl = st % 2
            dma("gpsimd", htb[sl][:],
                hT[:, st * 128:(st + 1) * 128].rearrange(
                    "(a p) t -> p a t", p=128), f_p4h[sl],
                waits=[slot_rd[sl]])
            lm = f_p4h[sl].at()
            dma("sync", csb[sl][:],
                cosT[st * 128:(st + 1) * 128, half * 512:(half + 1) * 512],
                f_p4c[sl], waits=[cs_rd[sl]])
            dma("sync", ssb[sl][:],
                sinT[st * 128:(st + 1) * 128, half * 512:(half + 1) * 512],
                f_p4c[sl])
            csm = f_p4c[sl].at()
            for hb in range(2):
                bk = pbank[hb % 2]
                for a in range(16):
                    mw = [lm, wm, bank_rd.get(id(bk))] if a == 0 else []
                    mm(bk[:], htb[sl][:, a, :],
                       wkr_sb[:, a, hb * 512:(hb + 1) * 512],
                       start=(a == 0), stop=(a == 15),
                       waits=mw, inc=(c_t if a == 15 else None))
                tm = c_t.at()
                vop(lambda e, bk=bk, hb=hb, half=half: e.tensor_tensor(
                    kr_t[:, hb * 512:(hb + 1) * 512], bk[:],
                    bkr_sb[:, half * 1024 + hb * 512:
                           half * 1024 + (hb + 1) * 512], OP.add),
                    waits=[tm, bm])
                bank_rd[id(bk)] = c_v.at()
            kr_mark = c_v.at()
            rope_tile(krr_t, kr_t, csb[sl], ssb[sl], rtmp, 512,
                      [kr_mark, csm, spill_last])
            rope_mark = c_v.at()
            cs_rd[sl] = rope_mark
            slot_rd[sl] = c_t.at()
            dma("sync", KRs[st * 128:(st + 1) * 128,
                            half * 1024:(half + 1) * 1024], krr_t[:], f_sp,
                waits=[rope_mark])
            spill_last = f_sp.at()
    kr_spill_mark = f_sp.at()
    if "kr" in dbg:
        dma("gpsimd", dbg_out["kr"][:], KRs[:], f_st, waits=[kr_spill_mark])
    sb.free("htb1", "htb0", "ssb1", "ssb0", "csb1", "csb0", "rtmp",
            "krr_t", "kr_t", "bkr_sb", "wkr_sb")

    arena_f = sb.alloc("arena_f", [128, 16512], F32)
    O_acc = arena_f[:, 0:16384].rearrange("p (a d) -> p a d", a=NT)
    lacc = arena_f[:, 16384:16512].rearrange("p (a h) -> p a h", a=NT)
    x1g = arena_f[:, 0:8192].rearrange("p (a d) -> p a d", a=4)
    r2g = arena_f[:, 8192:16384].rearrange("p (a d) -> p a d", a=4)
    kctb = [sb.alloc(f"kctb{i}", [128, 8, CH], BF16) for i in range(2)]
    krtb = [sb.alloc(f"krtb{i}", [128, 8, CH], BF16) for i in range(2)]
    vtb_b = sb.alloc("vtb", [128, 4, 1024], BF16)
    vtb = [vtb_b, vtb_b]
    P_sb = [sb.alloc(f"P_sb{i}", [128, CH], BF16) for i in range(2)]
    PT_sb = [sb.alloc(f"PT_sb{i}", [128, CH], BF16) for i in range(2)]
    ltmp = sb.alloc("ltmp", [128, 2], F32)
    f_at = [flow("f_at0"), flow("f_at1")]

    fm = fence()
    unit = 0
    psS = [pbank[0], pbank[1]]
    psAV = [pbank[2], pbank[3]]
    exp_done = {}
    p_free = [None, None]
    l_free = [None, None]
    pt_free = [None, None]
    av_evac = {}
    ld_consumed = [None, None]
    for c in range(NCH):
        for hg in range(2):
            par = (2 * c + hg) % 2
            for h8 in range(8):
                h = hg * 8 + h8
                dmaT(kctb[par][:, h8, :],
                     KCs[c * CH:(c + 1) * CH, h * 128:(h + 1) * 128],
                     f_at[par], waits=[ld_consumed[par], kv_spill_mark,
                                       kr_spill_mark] + (fm if c == 0 else []))
                dmaT(krtb[par][:, h8, :],
                     KRs[c * CH:(c + 1) * CH, h * 128:(h + 1) * 128],
                     f_at[par])
            dma("sync", vtb[par][:],
                Vs[c * CH:(c + 1) * CH,
                   hg * 1024:(hg + 1) * 1024].rearrange(
                       "(s p) d -> p s d", p=128), f_at[par],
                waits=[ld_consumed[1 - par]])
            ldm = f_at[par].at()
            for h8 in range(8):
                h = hg * 8 + h8
                for t in range(c, NT):
                    us = unit % 2
                    sbk = psS[us]
                    mm(sbk[:], qCT[:, h, t * 128:(t + 1) * 128],
                       kctb[par][:, h8, :], start=True, stop=False,
                       waits=[ldm, exp_done.get(id(sbk))])
                    mm(sbk[:], qRT[:, h, t * 128:(t + 1) * 128],
                       krtb[par][:, h8, :], start=False, stop=True, inc=c_t)
                    sm = c_t.at()
                    sop(lambda e, us=us, sbk=sbk: e.activation(
                        P_sb[us][:], sbk[:], AF.Exp, scale=SM_SCALE,
                        accum_out=ltmp[:, us:us + 1]),
                        waits=[sm, p_free[us], l_free[us]])
                    exp_done[id(sbk)] = c_s.at()
                    em = c_s.at()
                    lw = [em]
                    if t == c:
                        vop(lambda e, us=us: e.tensor_mul(
                            P_sb[us][:], P_sb[us][:], mask_sb[:]),
                            waits=[em, ld0])
                        vop(lambda e, us=us: e.reduce_sum(
                            ltmp[:, us:us + 1], P_sb[us][:], axis=AX))
                        lw = [c_v.at()]
                    lvm = c_v.at()
                    tb = psT[us]
                    for s2 in range(4):
                        transp(tb[:, s2 * 128:(s2 + 1) * 128],
                               P_sb[us][:, s2 * 128:(s2 + 1) * 128],
                               waits=([em, lvm, psT_rd[us]] if s2 == 0
                                      else []),
                               inc=(c_t if s2 == 3 else None))
                    trm = c_t.at()
                    p_free[us] = trm
                    vop(lambda e, us=us, tb=tb: e.tensor_copy(
                        PT_sb[us][:], tb[:, 0:512]),
                        waits=[trm, pt_free[us]])
                    psT_rd[us] = c_v.at()
                    ptm = c_v.at()
                    if c == 0:
                        vop(lambda e, t=t, h=h, us=us: e.tensor_copy(
                            lacc[:, t, h:h + 1], ltmp[:, us:us + 1]),
                            waits=lw)
                    else:
                        vop(lambda e, t=t, h=h, us=us: e.tensor_add(
                            lacc[:, t, h:h + 1], lacc[:, t, h:h + 1],
                            ltmp[:, us:us + 1]), waits=lw)
                    l_free[us] = c_v.at()
                    av = psAV[us]
                    for s2 in range(4):
                        mw = [ptm, av_evac.get(id(av))] if s2 == 0 else []
                        mm(av[:, 0:128],
                           PT_sb[us][:, s2 * 128:(s2 + 1) * 128],
                           vtb[par][:, s2, h8 * 128:(h8 + 1) * 128],
                           start=(s2 == 0), stop=(s2 == 3),
                           waits=mw, inc=(c_t if s2 == 3 else None))
                    avm = c_t.at()
                    pt_free[us] = avm
                    if c == 0:
                        vop(lambda e, t=t, h=h, av=av: e.tensor_copy(
                            O_acc[:, t, h * 128:(h + 1) * 128], av[:, 0:128]),
                            waits=[avm])
                    else:
                        vop(lambda e, t=t, h=h, av=av: e.tensor_add(
                            O_acc[:, t, h * 128:(h + 1) * 128],
                            O_acc[:, t, h * 128:(h + 1) * 128], av[:, 0:128]),
                            waits=[avm])
                    av_evac[id(av)] = c_v.at()
                    unit += 1
            ld_consumed[par] = c_t.at()
    attn_done = fence()
    if "lacc" in dbg:
        dma("sync", dbg_out["lacc"][:],
            lacc[:].rearrange("p a h -> p (a h)"), f_st, waits=attn_done)
    if "oacc" in dbg:
        for t in range(NT):
            dma("sync", dbg_out["oacc"][t * 128:(t + 1) * 128, :],
                O_acc[:, t, :], f_st, waits=attn_done)
    sb.free("ltmp", "PT_sb1", "PT_sb0", "P_sb1", "P_sb0",
            "vtb", "krtb1", "krtb0", "kctb1", "kctb0")

    sqs = sb.alloc("sqs", [128, D // 2], F32)
    bo_sb = sb.alloc("bo_sb", [128, D], F32)
    hob = [sb.alloc(f"hob{i}", [128, D], F32) for i in range(2)]
    o_t = sb.alloc("o_t", [128, D], BF16)
    oT_sb = sb.alloc("oT_sb", [128, 16, 128], BF16)
    x1_t = sb.alloc("x1_t", [128, D], F32)
    linv = sb.alloc("linv", [128, H], F32)
    f_wo = flow("f_wo")
    f_ob = flow("f_ob")
    f_ho = [flow("f_ho0"), flow("f_ho1")]
    if "nowoload" not in ablate:
        dma("gpsimd", wo_view[:],
            woT[:].rearrange("(a p) n -> p a n", p=128),
            f_wo, waits=attn_done)
    dma("sync", bo_sb[:], bo_bc[:], f_ob, waits=attn_done)
    wm = f_wo.at()
    bm = f_ob.at()
    ho_rd = [None, None]
    ln_state = {"sqs_rd": None}
    if "lnx" in dbg:
        lnsave = sb.alloc("lnsave", [128, NT, 8], F32)

    def layernorm(x_ap, r_mark, ln_out=None, scratch=None):
        sq = scratch if scratch is not None else sqs
        rm = r_mark if isinstance(r_mark, list) else [r_mark]
        h_ = D // 2
        vop(lambda e: e.tensor_mul(sq[:], x_ap[:, 0:h_], x_ap[:, 0:h_]),
            waits=rm)
        vop(lambda e: e.reduce_sum(lnc[:, 0:1], x_ap, axis=AX))
        vop(lambda e: e.reduce_sum(lnc[:, 2:3], sq[:], axis=AX))
        vop(lambda e: e.tensor_mul(sq[:], x_ap[:, h_:D], x_ap[:, h_:D]))
        vop(lambda e: e.reduce_sum(lnc[:, 6:7], sq[:], axis=AX))
        vop(lambda e: e.tensor_scalar(lnc[:, 1:2], lnc[:, 0:1], 1.0 / D,
                                      None, OP.mult))
        vop(lambda e: e.tensor_copy(epsc[:], epsc[:]))
        vop(lambda e: e.tensor_add(lnc[:, 2:3], lnc[:, 2:3], lnc[:, 6:7]))
        vop(lambda e: e.tensor_copy(epsc[:], epsc[:]))
        vop(lambda e: e.tensor_scalar(lnc[:, 3:4], lnc[:, 2:3], 1.0 / D,
                                      None, OP.mult))
        vop(lambda e: e.tensor_mul(lnc[:, 4:5], lnc[:, 1:2], lnc[:, 1:2]))
        vop(lambda e: e.tensor_copy(epsc[:], epsc[:]))
        vop(lambda e: e.tensor_sub(lnc[:, 3:4], lnc[:, 3:4], lnc[:, 4:5]))
        vop(lambda e: e.tensor_copy(epsc[:], epsc[:]))
        sop(lambda e: e.activation(lnc[:, 4:5], lnc[:, 3:4], AF.Sqrt,
                                   bias=epsc[:, 0:1]),
            waits=[c_v.at(), init_mark])
        vop(lambda e: e.tensor_copy(lnc[:, 6:7], lnc[:, 6:7]),
            waits=[c_s.at()])
        vop(lambda e: e.reciprocal(lnc[:, 5:6], lnc[:, 4:5]))
        vop(lambda e: e.tensor_copy(lnc[:, 7:8], lnc[:, 6:7]))
        vop(lambda e: e.tensor_scalar(ln_out if ln_out is not None else x_ap,
                                      x_ap, lnc[:, 1:2], lnc[:, 5:6],
                                      OP.subtract, OP.mult))
        return c_v.at()

    if "can" in dbg:
        can_words = (nc.sbuf_top - nc.sbuf_base) // 4 - 64
        canary = sb.alloc("canary", [128, can_words], F32)
        dbg_out["can"] = nc.dram_tensor("dbg_can", [128, can_words], F32,
                                        kind="ExternalOutput").ap()
        dbg_out["canbase"] = nc.dram_tensor("dbg_canbase", [1, 1], F32,
                                            kind="ExternalOutput").ap()
        can_base_addr = nc.sbuf_base - can_words * 4
        dbg_out["can2"] = nc.dram_tensor("dbg_can2", [128, can_words], F32,
                                         kind="ExternalOutput").ap()
        vop(lambda e: e.memset(canary[:], 1234.5), waits=fence())
        can_set = c_v.at()
        dma("sync", dbg_out["can2"][:], canary[:], f_st, waits=[can_set])
    otr_prev = None
    x1sp_prev = None
    x1dbg_prev = None
    tile_fence = []
    for t in range(NT):
        sl = t % 2
        dma("sync", hob[sl][:], h_own[t * 128:(t + 1) * 128, :], f_ho[sl],
            waits=[ho_rd[sl]] + (attn_done if t == 0 else []))
        hm = f_ho[sl].at()
        if "nowo" in ablate:
            for db in range(4):
                vop(lambda e, db=db, sl=sl: e.tensor_tensor(
                    x1_t[:, db * 512:(db + 1) * 512],
                    bo_sb[:, db * 512:(db + 1) * 512],
                    hob[sl][:, db * 512:(db + 1) * 512], OP.add),
                    waits=[hm, bm, x1sp_prev] + (attn_done if t == 0 else []))
            ho_rd[sl] = c_v.at()
            r_marks = [c_v.at()]
            x1m = layernorm(x1_t[:], r_marks) if "noln" not in ablate \
                else r_marks[0]
            dma("sync", X1s[t * 128:(t + 1) * 128, :], x1_t[:], f_sp,
                waits=[x1m])
            x1sp_prev = f_sp.at()
            if "can" in dbg:
                key = f"can3_{t}"
                dbg_out[key] = nc.dram_tensor("dbg_" + key,
                                              [128, canary.shape[1]], F32,
                                              kind="ExternalOutput").ap()
                dma("sync", dbg_out[key][:], canary[:], f_st,
                    waits=[x1sp_prev, can_set])
            continue
        vop(lambda e, t=t: e.reciprocal(linv[:], lacc[:, t, :]),
            waits=(attn_done if t == 0 else []) +
                  (tile_fence if "serial7b" in ablate else []))
        vop(lambda e: e.tensor_copy(lnc[:, 6:7], lnc[:, 6:7]))
        for h in range(H):
            vop(lambda e, t=t, h=h: e.tensor_scalar(
                o_t[:, h * 128:(h + 1) * 128],
                O_acc[:, t, h * 128:(h + 1) * 128],
                linv[:, h:h + 1], None, OP.mult),
                waits=([otr_prev] if h == 0 else []))
        om = c_v.at()
        if "o" in dbg:
            dma("gpsimd", dbg_out["o"][t * 128:(t + 1) * 128, :], o_t[:],
                f_st, waits=[om])
        transpose_block(lambda a: oT_sb[:, a, :], o_t, 16,
                        [om] + ([f_st.at()] if "o" in dbg else []))
        otr_prev = c_t.at()
        otm = c_v.at()
        if "ot" in dbg:
            dma("gpsimd", dbg_out["ot"][t * 128:(t + 1) * 128, :],
                oT_sb[:].rearrange("p a t -> p (a t)"), f_st, waits=[otm])
            otm_l = [otm, f_st.at()]
        else:
            otm_l = [otm]
        for db in range(4):
            bk = pbank[db]
            for a in range(16):
                mw = (otm_l + [wm, bank_rd.get(id(bk)), av_evac.get(id(bk))]
                      + attn_done) if a == 0 else []
                mm(bk[:], oT_sb[:, a, :],
                   wo_view[:, a, db * 512:(db + 1) * 512],
                   start=(a == 0), stop=(a == 15),
                   waits=mw, inc=(c_t if a == 15 else None))
            tm = c_t.at()
            vop(lambda e, bk=bk, db=db: e.tensor_tensor(
                x1_t[:, db * 512:(db + 1) * 512], bk[:],
                bo_sb[:, db * 512:(db + 1) * 512], OP.add),
                waits=[tm, bm, x1sp_prev, x1dbg_prev])
            if "noh" not in ablate:
                vop(lambda e, db=db, sl=sl: e.tensor_add(
                    x1_t[:, db * 512:(db + 1) * 512],
                    x1_t[:, db * 512:(db + 1) * 512],
                    hob[sl][:, db * 512:(db + 1) * 512]), waits=[hm])
            bank_rd[id(bk)] = c_v.at()
        ho_rd[sl] = c_v.at()
        r_marks = [c_v.at()]
        if "r1" in dbg:
            dma("gpsimd", dbg_out["r1"][t * 128:(t + 1) * 128, :], x1_t[:],
                f_st, waits=[c_v.at()])
            r_marks.append(f_st.at())
        if "noln" in ablate:
            x1m = r_marks[0]
        elif "lnsq" in ablate:
            sop(lambda e: e.activation(sqs[:], x1_t[:], AF.Square),
                waits=r_marks)
            rec.emit("vector", lambda e: e.nop(), waits=[c_s.at()], inc=c_v)
            x1m = c_v.at()
        elif "lnsq2" in ablate:
            sop(lambda e: e.activation(x1o_t[:], x1_t[:], AF.Square),
                waits=r_marks)
            rec.emit("vector", lambda e: e.nop(), waits=[c_s.at()], inc=c_v)
            x1m = c_v.at()
        elif "lnsq3" in ablate:
            sop(lambda e: e.activation(sqs[:, 0:512], x1_t[:, 0:512],
                                       AF.Square), waits=r_marks)
            rec.emit("vector", lambda e: e.nop(), waits=[c_s.at()], inc=c_v)
            x1m = c_v.at()
        elif "lnrd" in ablate:
            vop(lambda e: e.reduce_sum(lnc[:, 0:1], x1_t[:], axis=AX),
                waits=r_marks)
            vop(lambda e: e.tensor_copy(lnc[:, 6:7], lnc[:, 6:7]))
            x1m = c_v.at()
        else:
            x1m = layernorm(x1_t[:], r_marks)
            if "lnx" in dbg:
                vop(lambda e, t=t: e.tensor_copy(lnsave[:, t, :], lnc[:]))
        dma("sync", X1s[t * 128:(t + 1) * 128, :], x1_t[:], f_sp,
            waits=[x1m])
        x1sp_prev = f_sp.at()
        if "can" in dbg and t == 0:
            dbg_out["can3"] = nc.dram_tensor("dbg_can3", canary.shape[1:2]
                                             and [128, canary.shape[1]], F32,
                                             kind="ExternalOutput").ap()
            dma("sync", dbg_out["can3"][:], canary[:], f_st,
                waits=[x1sp_prev, can_set])
        tile_fence = fence() + [x1sp_prev]
        if "x1" in dbg:
            dma("gpsimd", dbg_out["x1"][t * 128:(t + 1) * 128, :], x1_t[:],
                f_st, waits=[x1m])
            x1dbg_prev = f_st.at()
    x1_all = f_sp.at()
    if "wod" in dbg:
        dma("gpsimd", dbg_out["wod"][:], arena_b[:], f_st, waits=fence())
    if "lnx" in dbg:
        dma("sync", dbg_out["lnx"][:],
            lnsave[:].rearrange("p a b -> p (a b)"), f_st, waits=[c_v.at()])
    if "x1s" in dbg:
        dma("gpsimd", dbg_out["x1s"][:], X1s[:], f_st, waits=[x1_all])
    wo_done = fence()
    if "can" in dbg:
        sb.free("canary")
    if "lnx" in dbg:
        sb.free("lnsave")
    sb.free("linv", "x1_t", "oT_sb", "o_t", "hob1", "hob0",
            "bo_sb", "sqs")

    for k, shp, dt in (("xqt", [128, 8192], BF16), ("h3r", [128, 32768], BF16),
                       ("h3q", [128, 32768], BF16), ("r2", [128, 8192], F32),
                       ("sr8", [128, 24], F32)):
        if k in dbg:
            dbg_out[k] = nc.dram_tensor("dbg_" + k, shp,
                                        dt if k not in ("r2", "sr8") else F32,
                                        kind="ExternalOutput").ap()
    srow = sb.alloc("srow", [128, 4, 6], F32)
    g3c = sb.alloc("g3c", [128, 4, 2], F32)
    s3xb = sb.alloc("s3xb", [128, 128], F32)
    s3col = sb.alloc("s3col", [128, 128], BF16)
    sqs8 = sb.alloc("sqs8", [128, D // 2], F32)
    vop(lambda e: e.memset(s3col[:], 0.0), waits=wo_done)

    for g2 in range(2 if not no_ffn else 0):
        xqT = sb.alloc("xqT", [128, 4, 16, 128], BF16)
        xq_t = sb.alloc("xq_t", [128, D], BF16)
        h3b = sb.alloc("h3b", [128, CH], BF16)
        gv1 = sb.alloc("gv1", [128, CH], F32)
        gv2 = sb.alloc("gv2", [128, CH], F32)
        gv3 = sb.alloc("gv3", [128, CH], F32)
        gsave = sb.alloc("gsave", [128, 4, CH], BF16)
        tsl = sb.alloc("tsl", [128, 2, 4, CH], BF16)
        bgb = sb.alloc("bgb", [128, CH], F32)
        bvb = sb.alloc("bvb", [128, CH], F32)
        f_x1g = flow(f"f_x1g_{g2}")
        f_slab = flow(f"f_slab_{g2}")
        f_tob = flow(f"f_tob_{g2}")

        dma("sync", x1g[:],
            X1s[g2 * 512:(g2 + 1) * 512, :].rearrange(
                "(a p) d -> p a d", p=128),
            f_x1g, waits=[x1_all] + (wo_done if g2 == 0 else fence()))
        x1lm = f_x1g.at()
        xq_free = None
        for i in range(4):
            vop(lambda e, i=i: e.tensor_reduce(
                srow[:, i, 0:1], x1g[:, i, :], axis=AX, op=OP.max,
                apply_absolute_value=True), waits=[x1lm])
        for i in range(4):
            vop(lambda e, i=i: e.tensor_scalar(
                srow[:, i, 0:1], srow[:, i, 0:1], 1e-5, None, OP.max))
        for i in range(4):
            vop(lambda e, i=i: e.reciprocal(srow[:, i, 1:2],
                                            srow[:, i, 0:1]))
        for i in range(4):
            vop(lambda e, i=i: e.tensor_scalar(
                srow[:, i, 1:2], srow[:, i, 1:2], 127.0, None, OP.mult))
        for i in range(4):
            vop(lambda e, i=i: e.tensor_mul(srow[:, i, 2:3],
                                            srow[:, i, 0:1], gam[:, 6:7]),
                waits=[gam_mark])
            vop(lambda e, i=i: e.tensor_mul(srow[:, i, 3:4],
                                            srow[:, i, 0:1], gam[:, 7:8]))
        for i in range(4):
            for hf in range(2):
                vop(lambda e, i=i, hf=hf: e.tensor_scalar(
                    sqs8[:], x1g[:, i, hf * 1024:(hf + 1) * 1024],
                    srow[:, i, 1:2], MAGIC, OP.mult, OP.add))
                vop(lambda e, hf=hf: e.tensor_scalar(
                    xq_t[:, hf * 1024:(hf + 1) * 1024], sqs8[:], MAGIC, None,
                    OP.subtract), waits=[xq_free] if hf == 0 else [])
            xqm = c_v.at()
            transpose_block(lambda a, i=i: xqT[:, i, a, :], xq_t, 16, [xqm])
            xq_free = c_t.at()
        xq_mark = c_v.at()
        if "xqt" in dbg and g2 == 0:
            dma("gpsimd", dbg_out["xqt"][:],
                xqT[:].rearrange("p i a t -> p (i a t)"), f_st,
                waits=[xq_mark, xq_free])

        slab_rd = None
        silu_prev = None
        h3b_free = None
        slab_ring = [0]
        slab_free = {0: None, 1: None}
        for hb in range(HID // CH):
            dma("sync", bgb[:], bg_bc[:, hb * CH:(hb + 1) * CH], f_slab,
                waits=[slab_rd] + quant_done_sp)
            dma("sync", bvb[:], bv_bc[:, hb * CH:(hb + 1) * CH], f_slab)
            for half, TS_ in ((0, TgS), (1, TgS), (2, TvS), (3, TvS)):
                hh = half % 2
                for sub in range(2):
                    sl2 = slab_ring[0]
                    slab_ring[0] ^= 1
                    dma("sync", tsl[:, sl2, :, :],
                        TS_[hh * 1024 + sub * 512:
                            hh * 1024 + (sub + 1) * 512,
                            hb * CH:(hb + 1) * CH].rearrange(
                            "(a p) n -> p a n", p=128), f_slab,
                        waits=[slab_free[sl2]] +
                              ([x1_all] if (g2 == 0 and hb == 0 and
                                            half == 0 and sub == 0)
                               else []))
                    slm = f_slab.at()
                    for i in range(4):
                        bk = pbank[i]
                        for a2 in range(4):
                            a = hh * 8 + sub * 4 + a2
                            mw = ([slm, xq_mark, bank_rd.get(id(bk)),
                                   av_evac.get(id(bk))] if a2 == 0 else [])
                            mm(bk[:], xqT[:, i, a, :], tsl[:, sl2, a2, :],
                               start=(a == 0), stop=(a == 15),
                               waits=mw,
                               inc=(c_t if (a == 15 or
                                            (i == 3 and a2 == 3))
                                    else None))
                    slab_free[sl2] = c_t.at()
                if half == 1:
                    for i in range(4):
                        bk = pbank[i]
                        vop(lambda e, bk=bk, i=i: e.scalar_tensor_tensor(
                            gv1[:], bk[:], srow[:, i, 2:3], bgb[:],
                            OP.mult, OP.add),
                            waits=[c_t.at(), silu_prev])
                        bank_rd[id(bk)] = c_v.at()
                        sop(lambda e: e.activation(gv2[:], gv1[:], AF.Silu),
                            waits=[c_v.at()])
                        silu_prev = c_s.at()
                        vop(lambda e, i=i: e.tensor_copy(gsave[:, i, :],
                                                         gv2[:]),
                            waits=[silu_prev])
            for i in range(4):
                bk = pbank[i]
                vop(lambda e, bk=bk, i=i: e.scalar_tensor_tensor(
                    gv3[:], bk[:], srow[:, i, 3:4], bvb[:],
                    OP.mult, OP.add), waits=[c_t.at(), h3b_free])
                bank_rd[id(bk)] = c_v.at()
                vop(lambda e, i=i: e.tensor_mul(h3b[:], gsave[:, i, :],
                                                gv3[:]))
                vop(lambda e, i=i: e.tensor_reduce(
                    g3c[:, i, 1:2], h3b[:], axis=AX, op=OP.max,
                    apply_absolute_value=True))
                h3m = c_v.at()
                transpose_block(
                    lambda s2, i=i, hb=hb: h3T[:, i, hb * 4 + s2, :],
                    h3b, 4, [h3m] + (wo_done if (g2 == 0 and hb == 0)
                                     else []))
                h3b_free = c_t.at()
                if hb == 0:
                    vop(lambda e, i=i: e.tensor_copy(g3c[:, i, 0:1],
                                                     g3c[:, i, 1:2]))
                else:
                    vop(lambda e, i=i: e.tensor_max(
                        g3c[:, i, 0:1], g3c[:, i, 0:1], g3c[:, i, 1:2]))
            slab_rd = c_t.at()
        h3_mark = c_v.at()
        h3tap = []
        if "h3r" in dbg and g2 == 0:
            dma("gpsimd", dbg_out["h3r"][:],
                arena_b[:, 0:32768], f_st, waits=[h3_mark])
            h3tap = [f_st.at()]
        sb.free("bvb", "bgb", "tsl", "gsave", "gv3", "gv2", "gv1", "h3b",
                "xq_t", "xqT")

        qtmp = sb.alloc("qtmp", [128, 2, 128], F32)
        for i in range(4):
            vop(lambda e, i=i: e.tensor_scalar(
                g3c[:, i, 0:1], g3c[:, i, 0:1], 1e-5, None, OP.max),
                waits=[h3_mark])
        for i in range(4):
            vop(lambda e, i=i: e.reciprocal(g3c[:, i, 1:2], g3c[:, i, 0:1]))
            vop(lambda e, i=i: e.tensor_mul(srow[:, i, 2:3], g3c[:, i, 0:1],
                                            gam[:, 8:9]))
        for i in range(4):
            vop(lambda e, i=i: e.tensor_scalar(
                g3c[:, i, 1:2], g3c[:, i, 1:2], 127.0, None, OP.mult))
        for i in range(4):
            tb = psT[0]
            vop(lambda e, i=i: e.tensor_copy(s3col[:, 0:1], g3c[:, i, 1:2]))
            vop(lambda e: e.tensor_copy(epsc[:], epsc[:]))
            transp(tb[:, 0:128], s3col[:], waits=[c_v.at(), psT_rd[0]],
                   inc=c_t)
            vop(lambda e, tb=tb: e.tensor_copy(s3row[0:1, :],
                                               tb[0:1, 0:128]),
                waits=[c_t.at()])
            psT_rd[0] = c_v.at()
            mm(pbank[0][:, 0:128], onesrow_bf[:], s3row[0:1, :],
               start=True, stop=True,
               waits=[c_v.at(), bank_rd.get(id(pbank[0])),
                      av_evac.get(id(pbank[0]))], inc=c_t)
            vop(lambda e: e.tensor_copy(s3xb[:], pbank[0][:, 0:128]),
                waits=[c_t.at()])
            bank_rd[id(pbank[0])] = c_v.at()
            vop(lambda e: e.tensor_copy(lnc[:, 6:7], lnc[:, 6:7]))
            s3bm = c_v.at()
            for a in range(64):
                vop(lambda e, i=i, a=a: e.tensor_mul(
                    qtmp[:, a % 2, :], h3T[:, i, a, :], s3xb[:]),
                    waits=([s3bm] + h3tap) if a == 0 else [])
                if a > 0:
                    vop(lambda e, i=i, a=a: e.tensor_scalar(
                        h3T[:, i, a - 1, :], qtmp[:, (a - 1) % 2, :],
                        MAGIC, MAGIC, OP.add, OP.subtract))
            vop(lambda e: e.tensor_copy(epsc[:], epsc[:]))
            vop(lambda e, i=i: e.tensor_scalar(
                h3T[:, i, 63, :], qtmp[:, 1, :], MAGIC, MAGIC, OP.add,
                OP.subtract))
        h3q_mark = c_v.at()
        h3qtap = []
        if "h3q" in dbg and g2 == 0:
            dma("gpsimd", dbg_out["h3q"][:],
                arena_b[:, 0:32768], f_st, waits=[h3q_mark])
            h3qtap = [f_st.at()]
        if "sr8" in dbg and g2 == 0:
            dma("gpsimd", dbg_out["sr8"][:, 0:24],
                srow[:].rearrange("p i c -> p (i c)"), f_st,
                waits=[h3q_mark])
            h3qtap = h3qtap + [f_st.at()]
        sb.free("qtmp")

        DBW = 256
        tob = sb.alloc("tob", [128, 64, DBW], BF16)
        bob = sb.alloc("bob", [128, DBW], F32)
        to_rd = None
        for db in range(D // DBW):
            dma("sync", tob[:],
                ToS[:, db * DBW:(db + 1) * DBW].rearrange(
                    "(a p) n -> p a n", p=128),
                f_tob, waits=[to_rd] + quant_done_sp +
                             ([h3q_mark, c_t.at()] if db == 0 else []))
            dma("sync", bob[:], bout_bc[:, db * DBW:(db + 1) * DBW], f_tob)
            tolm = f_tob.at()
            for i in range(4):
                bk = pbank[i % 4]
                for a in range(64):
                    mw = ([tolm, h3q_mark, bank_rd.get(id(bk)),
                           av_evac.get(id(bk))] + h3qtap) if a == 0 else []
                    mm(bk[:, 0:DBW], h3T[:, i, a, :], tob[:, a, :],
                       start=(a == 0), stop=(a == 63),
                       waits=mw, inc=(c_t if a == 63 else None))
                tm = c_t.at()
                vop(lambda e, bk=bk, i=i, db=db: e.scalar_tensor_tensor(
                    r2g[:, i, db * DBW:(db + 1) * DBW], bk[:, 0:DBW],
                    srow[:, i, 2:3], bob[:], OP.mult, OP.add),
                    waits=[tm])
                bank_rd[id(bk)] = c_v.at()
            to_rd = c_t.at()
        r2tap = []
        if "r2" in dbg and g2 == 0:
            r2m = c_v.at()
            dma("gpsimd", dbg_out["r2"][:],
                r2g[:].rearrange("p i d -> p (i d)"), f_st, waits=[r2m])
            r2tap = [f_st.at()]
        for i in range(4):
            t = g2 * 4 + i
            vop(lambda e, i=i: e.tensor_add(r2g[:, i, :], r2g[:, i, :],
                                            x1g[:, i, :]),
                waits=[x1lm] + r2tap)
            om = layernorm(r2g[:, i, :], c_v.at(), scratch=sqs8)
            dma("sync", OUT[t * 128:(t + 1) * 128, :], r2g[:, i, :], f_st,
                waits=[om])
        sb.free("bob", "tob")

    final_waits = [fl.at() for fl in flows.values()]
    rec.emit("sync", lambda e: e.nop(), waits=final_waits)
    rec.emit("sync", lambda e: e.sem_clear(range(SEM_LO, SEM_HI)))

    with nc.Block() as block:
        rec.replay(block)
    ctx.close()
    return nc



def _rope_tables():
    idx = np.arange(DH // 2, dtype=np.float32)
    pos = np.arange(S, dtype=np.float32)
    freq = np.float32(10000.0) ** (np.float32(-2.0) * idx / np.float32(DH))
    theta = pos[:, None].astype(np.float32) * freq[None, :].astype(np.float32)
    cos = np.cos(theta.astype(np.float64)).astype(np.float32)
    sin = np.sin(theta.astype(np.float64)).astype(np.float32)
    cosT = np.ascontiguousarray(np.tile(cos, (1, H))).astype(BF)
    sinT = np.ascontiguousarray(np.tile(sin, (1, H))).astype(BF)
    return cosT, sinT


def kernel(
    L0_attn_h, L0_attn_W_DKV_w, L0_attn_W_DKV_b, L0_attn_W_UK_w,
    L0_attn_W_UK_b, L0_attn_W_UV_w, L0_attn_W_UV_b, L0_attn_W_DQ_w,
    L0_attn_W_DQ_b, L0_attn_W_UQ_w, L0_attn_W_UQ_b, L0_attn_W_KR_w,
    L0_attn_W_KR_b, L0_attn_W_O_w, L0_attn_W_O_b, L0_ffn_gate_w,
    L0_ffn_gate_b, L0_ffn_val_w, L0_ffn_val_b, L0_ffn_out_w, L0_ffn_out_b,
    _debug=(), _cores=None, _no_ffn=False, _ablate=(),
):
    f32 = np.float32
    h = np.asarray(L0_attn_h, f32)

    def T(w):
        return np.ascontiguousarray(np.asarray(w, f32).T)

    def bc(b):
        bb = np.asarray(b, f32)
        return np.ascontiguousarray(
            np.broadcast_to(bb[None, :], (128, bb.shape[0])))

    def col(b, n):
        return np.ascontiguousarray(np.asarray(b, f32).reshape(n, 128).T)

    shared = {
        "wdkvT": T(L0_attn_W_DKV_w),
        "bdkv_col": col(L0_attn_W_DKV_b, DC // 128),
        "wdqT": T(L0_attn_W_DQ_w), "bdq_col": col(L0_attn_W_DQ_b, DQC // 128),
        "wuqT": T(L0_attn_W_UQ_w), "buq_bc": bc(L0_attn_W_UQ_b),
        "wukT": T(L0_attn_W_UK_w), "buk_bc": bc(L0_attn_W_UK_b),
        "wuvT": T(L0_attn_W_UV_w), "buv_bc": bc(L0_attn_W_UV_b),
        "wkrT": T(L0_attn_W_KR_w), "bkr_bc": bc(L0_attn_W_KR_b),
        "woT": T(L0_attn_W_O_w), "bo_bc": bc(L0_attn_W_O_b),
        "wgT": T(L0_ffn_gate_w), "bg_bc": bc(L0_ffn_gate_b),
        "wvT": T(L0_ffn_val_w), "bv_bc": bc(L0_ffn_val_b),
        "woutT": T(L0_ffn_out_w), "bout_bc": bc(L0_ffn_out_b),
        "identI": np.eye(128, dtype=BF),
    }
    cosT, sinT = _rope_tables()
    shared["cosT"] = cosT
    shared["sinT"] = sinT

    hTb = [np.ascontiguousarray(h[b].T) for b in range(B)]

    in_maps = []
    ncores = 8 if _cores is None else _cores
    for core in range(ncores):
        b, j = core // 4, core % 4
        own = np.concatenate(
            [np.arange(128 * (j + 4 * t), 128 * (j + 4 * t) + 128)
             for t in range(NT)])
        mask = (np.arange(CH)[None, :] <= 128 * j + np.arange(128)[:, None])
        m = dict(shared)
        m["hT"] = hTb[b]
        m["hTown"] = np.ascontiguousarray(hTb[b][:, own])
        m["h_own"] = np.ascontiguousarray(h[b][own])
        m["cos_own"] = np.ascontiguousarray(cosT[own])
        m["sin_own"] = np.ascontiguousarray(sinT[own])
        m["maskI"] = mask.astype(BF)
        in_maps.append(m)

    nc = build_program(debug=_debug, no_ffn=_no_ffn, ablate=_ablate)
    res = run_bass_kernel_spmd(nc, in_maps, core_ids=list(range(ncores)))

    out = np.zeros((B, S, D), np.float32)
    for core in range(ncores):
        b, j = core // 4, core % 4
        co = res.results[core]["out"]
        for t in range(NT):
            out[b, 128 * (j + 4 * t):128 * (j + 4 * t) + 128] = \
                co[128 * t:128 * t + 128]
    if _debug:
        return out, res
    return out


# version 05 (final)
# speedup vs baseline: 1.1698x; 1.0599x over previous
import numpy as np
import ml_dtypes
from contextlib import ExitStack
import concourse.bass as bass
import concourse.mybir as mybir
from concourse.bass_utils import run_bass_kernel_spmd

F32 = mybir.dt.float32
BF16 = mybir.dt.bfloat16
AX = mybir.AxisListType.X
OP = mybir.AluOpType
AF = mybir.ActivationFunctionType

B, S, D = 2, 4096, 2048
H, DH = 16, 128
DC, DQC = 512, 1024
HID = 8192
NT = 8
NTOK = NT * 128
ST = S // 128
CH = 512
NCH = S // CH
MAGIC = 3 * 2.0 ** 22
SM_SCALE = 1.0 / 16.0
LN_EPS = 1e-5
BF = ml_dtypes.bfloat16

ENGS = ("tensor", "vector", "scalar", "gpsimd", "sync")


class Flow:
    __slots__ = ("sem", "count", "name")

    def __init__(self, sem, name=""):
        self.sem = sem
        self.count = 0
        self.name = name

    def at(self):
        return (self, self.count)


class Rec:
    def __init__(self, nc):
        self.nc = nc
        self.lists = {k: [] for k in ENGS}
        self.last_wait = {}

    def emit(self, eng, fn, waits=(), inc=None, inc_n=1):
        wl = []
        for w in waits:
            if w is None:
                continue
            fl, val = w
            if val <= 0:
                continue
            key = (eng, id(fl))
            if self.last_wait.get(key, -1) >= val:
                continue
            self.last_wait[key] = val
            wl.append((fl.sem, val))
        sem = None
        if inc is not None:
            inc.count += inc_n
            sem = inc.sem

        def thunk(e, wl=wl, fn=fn, sem=sem, n=inc_n):
            for s, v in wl:
                e.wait_ge(s, v)
            instr = fn(e)
            if sem is not None:
                instr.then_inc(sem, n)

        self.lists[eng].append(thunk)

    def replay(self, block):
        for eng in ENGS:
            lst = self.lists[eng]
            getattr(block, eng)(lambda e, lst=lst: [f(e) for f in lst])


class SBAlloc:
    def __init__(self, nc):
        self.nc = nc
        self.stack = []
        self.serial = 0

    def alloc(self, name, shape, dt):
        self.serial += 1
        g = self.nc.sbuf_tensor(f"{name}__{self.serial}", list(shape), dt)
        h = g.__enter__()
        self.stack.append((name, g))
        return h

    def free(self, *names):
        want = set(names)
        for _ in range(len(want)):
            n, g = self.stack.pop()
            assert n in want, f"free order violation: {n} vs {want}"
            want.discard(n)
            g.__exit__(None, None, None)


def build_program(debug=(), no_ffn=False, ablate=()):
    nc = bass.Bass("TRN2", target_bir_lowering=False)
    nc.sbuf_top = 199680
    dbg = set(debug)

    def din(name, shape, dt=F32):
        return nc.dram_tensor(name, list(shape), dt, kind="ExternalInput").ap()

    hT = din("hT", [D, S])
    hTown = din("hTown", [D, NTOK])
    h_own = din("h_own", [NTOK, D])
    wdkvT = din("wdkvT", [D, DC])
    bdkv_col = din("bdkv_col", [128, DC // 128])
    wdqT = din("wdqT", [D, DQC])
    bdq_col = din("bdq_col", [128, DQC // 128])
    wuqT = din("wuqT", [DQC, D])
    buq_bc = din("buq_bc", [128, D])
    wukT = din("wukT", [DC, D])
    buk_bc = din("buk_bc", [128, D])
    wuvT = din("wuvT", [DC, D])
    buv_bc = din("buv_bc", [128, D])
    wkrT = din("wkrT", [D, D])
    bkr_bc = din("bkr_bc", [128, D])
    woT = din("woT", [D, D])
    bo_bc = din("bo_bc", [128, D])
    wgT = din("wgT", [D, HID])
    bg_bc = din("bg_bc", [128, HID])
    wvT = din("wvT", [D, HID])
    bv_bc = din("bv_bc", [128, HID])
    woutT = din("woutT", [HID, D])
    bout_bc = din("bout_bc", [128, D])
    cosT = din("cosT", [S, 1024], BF16)
    sinT = din("sinT", [S, 1024], BF16)
    cos_own = din("cos_own", [NTOK, 1024], BF16)
    sin_own = din("sin_own", [NTOK, 1024], BF16)
    identI = din("identI", [128, 128], BF16)
    maskI = din("maskI", [128, CH], BF16)

    OUT = nc.dram_tensor("out", [NTOK, D], F32, kind="ExternalOutput").ap()

    def scrt(name, shape, dt=BF16):
        return nc.dram_tensor(name, list(shape), dt, kind="Internal").ap()

    KCs = scrt("KCs", [S, D])
    KRs = scrt("KRs", [S, D])
    Vs = scrt("Vs", [S, D])
    X1s = scrt("X1s", [NTOK, D], F32)
    TgS = scrt("TgS", [D, HID])
    TvS = scrt("TvS", [D, HID])
    ToS = scrt("ToS", [HID, D])

    dbg_out = {}
    if dbg:
        shapes = {
            "ckv": [DC, S], "cq": [DQC, NTOK], "qc": [NTOK, D],
            "qr": [NTOK, D], "kr": [S, D], "kc": [S, D], "v": [S, D],
            "gam": [128, 12], "o": [NTOK, D], "x1": [NTOK, D],
            "tg": [D, HID], "lacc": [128, NT * H], "oacc": [NTOK, D],
            "r1": [NTOK, D], "ot": [NTOK, D], "x1s": [NTOK, D],
            "lnx": [128, NT * 8], "wod": [128, 32768],
        }
        for k in dbg:
            if k in shapes:
                dbg_out[k] = nc.dram_tensor("dbg_" + k, shapes[k], F32,
                                            kind="ExternalOutput").ap()

    rec = Rec(nc)
    sb = SBAlloc(nc)
    ctx = ExitStack()

    flows = {}

    def flow(name):
        if name not in flows:
            sem = ctx.enter_context(nc.semaphore(name))
            flows[name] = Flow(sem, name)
        return flows[name]

    boot = flow("boot")
    f_ld = flow("f_ld")
    f_g = flow("f_g")
    f_sp = flow("f_sp")
    f_st = flow("f_st")
    c_t = flow("c_t")
    c_v = flow("c_v")
    c_s = flow("c_s")

    pbank = [ctx.enter_context(nc.psum_tensor(f"pb{i}", [128, 512], F32))
             for i in range(4)]
    psT = [ctx.enter_context(nc.psum_tensor(f"psT{i}", [128, 1024], BF16))
           for i in range(2)]

    def dma(queue, out, in_, fl, waits=()):
        rec.emit(queue, lambda e, o=out, i=in_: e.dma_start(out=o, in_=i),
                 waits=waits, inc=fl, inc_n=16)

    def dmaT(out, in_, fl, waits=()):
        rec.emit("sync",
                 lambda e, o=out, i=in_: e.dma_start(out=o, in_=i,
                                                     transpose=True),
                 waits=waits, inc=fl, inc_n=16)

    def mm(out, lhsT, rhs, start, stop, waits=(), inc=None):
        rec.emit("tensor",
                 lambda e, o=out, l=lhsT, r=rhs, a=start, b=stop:
                 e.matmul(o, l, r, start=a, stop=b),
                 waits=waits, inc=inc)

    def transp(out, in_, waits=(), inc=None):
        rec.emit("tensor",
                 lambda e, o=out, i=in_: e.transpose(o, i, ident_sb[:]),
                 waits=waits, inc=inc)

    def vop(fn, waits=()):
        rec.emit("vector", fn, waits=waits, inc=c_v)

    def sop(fn, waits=()):
        rec.emit("scalar", fn, waits=waits, inc=c_s)

    def fence():
        return [c_t.at(), c_v.at(), c_s.at()]

    bank_rd = {}

    SEM_LO, SEM_HI = 155, 256
    for _eng in ENGS:
        rec.emit(_eng, lambda e: e.sem_clear(range(SEM_LO, SEM_HI)))

    ident_sb = sb.alloc("ident_sb", [128, 128], BF16)
    mask_sb = sb.alloc("mask_sb", [128, CH], BF16)
    epsc = sb.alloc("epsc", [128, 1], F32)
    onescol = sb.alloc("onescol", [128, 1], F32)
    onesrow = sb.alloc("onesrow", [1, 128], F32)
    onesrow_bf = sb.alloc("onesrow_bf", [1, 128], BF16)
    s3row = sb.alloc("s3row", [1, 128], BF16)
    gam = sb.alloc("gam", [128, 12], F32)
    lnc = sb.alloc("lnc", [128, 8], F32)

    f_const = flow("f_const")
    dma("sync", ident_sb[:], identI[:], f_const)
    dma("sync", mask_sb[:], maskI[:], f_const)
    vop(lambda e: e.memset(epsc[:], LN_EPS))
    vop(lambda e: e.memset(onescol[:], 1.0))
    vop(lambda e: e.memset(onesrow[:], 1.0))
    vop(lambda e: e.memset(onesrow_bf[:], 1.0))
    init_mark = c_v.at()
    ld0 = f_const.at()

    arena_b = sb.alloc("arena_b", [128, 32768], BF16)
    qCT = arena_b[:, 0:16384].rearrange("p (h t) -> p h t", h=H)
    qRT = arena_b[:, 16384:32768].rearrange("p (h t) -> p h t", h=H)
    wo_view = arena_b[:, 0:32768].rearrange("p (a n) -> p a n", a=16)
    h3T = arena_b[:, 0:32768].rearrange("p (i a t) -> p i a t", i=4, a=64)
    ckvT = arena_b[:, 0:16384].rearrange("p (a t) -> p a t", a=DC // 128)

    WMATS = [(wgT, D, HID, TgS, 0), (wvT, D, HID, TvS, 1),
             (woutT, HID, D, ToS, 2)]
    gbuf = [sb.alloc(f"gbuf{i}", [128, 2048], F32) for i in range(2)]
    f_gb = [flow("f_gb0"), flow("f_gb1")]
    gacc = sb.alloc("gacc", [128, 4], F32)
    gtmp = sb.alloc("gtmp", [128, 2], F32)
    aspc = sb.alloc("aspc", [128, 1], F32)
    vop(lambda e: e.memset(aspc[:], 0.0))
    aspc_init = c_v.at()

    slot_rd = [None, None]
    for wT, nr, ncol, _, m in WMATS:
        vop(lambda e, m=m: e.memset(gacc[:, m:m + 1], 0.0),
            waits=[c_t.at()])
        idx = 0
        nch = (nr // 128) * (ncol // 2048)
        for a in range(nr // 128):
            for cb in range(ncol // 2048):
                sl = idx % 2
                dma("gpsimd", gbuf[sl][:],
                    wT[a * 128:(a + 1) * 128, cb * 2048:(cb + 1) * 2048],
                    f_gb[sl], waits=[slot_rd[sl]])
                lm = f_gb[sl].at()
                vop(lambda e, sl=sl: e.reduce_sum(
                    gtmp[:, sl:sl + 1], gbuf[sl][:], axis=AX,
                    apply_absolute_value=True), waits=[lm])
                if idx > 0:
                    psl = (idx - 1) % 2
                    vop(lambda e, psl=psl, m=m: e.tensor_add(
                        gacc[:, m:m + 1], gacc[:, m:m + 1],
                        gtmp[:, psl:psl + 1]))
                slot_rd[sl] = c_v.at()
                idx += 1
        vop(lambda e: e.tensor_copy(gtmp[:, 0:1], gtmp[:, 0:1]))
        vop(lambda e, m=m: e.tensor_add(
            gacc[:, m:m + 1], gacc[:, m:m + 1],
            gtmp[:, (nch - 1) % 2:(nch - 1) % 2 + 1]))
        mm(pbank[3][0:1, m:m + 1], gacc[:, m:m + 1], onescol[:],
           start=True, stop=True,
           waits=[c_v.at(), init_mark, bank_rd.get(id(pbank[3]))], inc=c_t)
        vop(lambda e, m=m, n=float(nr * ncol): e.tensor_scalar(
            gam[0:1, 9 + m:10 + m], pbank[3][0:1, m:m + 1],
            1.0 / n, 1e-5, OP.mult, OP.max), waits=[c_t.at()])
        bank_rd[id(pbank[3])] = c_v.at()
        mm(pbank[3][:, 256 + m:257 + m], onesrow[:], gam[0:1, 9 + m:10 + m],
           start=True, stop=True, waits=[c_v.at()], inc=c_t)
        vop(lambda e, m=m: e.tensor_copy(gam[:, m:m + 1],
                                         pbank[3][:, 256 + m:257 + m]),
            waits=[c_t.at()])
        bank_rd[id(pbank[3])] = c_v.at()
        vop(lambda e: e.tensor_copy(lnc[:, 6:7], lnc[:, 6:7]))
        vop(lambda e, m=m: e.reciprocal(gam[:, 3 + m:4 + m], gam[:, m:m + 1]))
        vop(lambda e, m=m: e.tensor_scalar(gam[:, 6 + m:7 + m],
                                           gam[:, m:m + 1],
                                           1.0 / 127.0, None, OP.mult))
    gam_mark = c_v.at()
    if "gam" in dbg:
        vop(lambda e: e.tensor_copy(gam[:, 9:12], gacc[:, 0:3]),
            waits=[gam_mark, c_t.at()])
        dma("sync", dbg_out["gam"][:], gam[:], f_st, waits=[c_v.at()])

    qout = [sb.alloc(f"qout{i}", [128, 2048], BF16) for i in range(2)]
    f_tw = [flow("f_tw0"), flow("f_tw1")]
    slot_rd = [None, None]
    out_wr = [None, None]
    act_rd = [None, None]
    for wT, nr, ncol, Tdst, m in WMATS:
        idx = 0
        for a in range(nr // 128):
            for cb in range(ncol // 2048):
                sl = idx % 2
                dma("sync", gbuf[sl][:],
                    wT[a * 128:(a + 1) * 128, cb * 2048:(cb + 1) * 2048],
                    f_gb[sl], waits=[slot_rd[sl], act_rd[sl], gam_mark])
                lm = f_gb[sl].at()
                vop(lambda e, sl=sl, m=m: e.tensor_scalar(
                    gbuf[sl][:], gbuf[sl][:], gam[:, 3 + m:4 + m], MAGIC,
                    OP.mult, OP.add), waits=[lm, out_wr[sl]])
                vop(lambda e, sl=sl: e.tensor_scalar(
                    gbuf[sl][:], gbuf[sl][:], MAGIC + 1.0, MAGIC - 1.0,
                    OP.min, OP.max))
                sop(lambda e, sl=sl: e.activation(
                    qout[sl][:], gbuf[sl][:], AF.Copy, bias=-MAGIC),
                    waits=[c_v.at(), out_wr[sl], init_mark] +
                          ([aspc_init] if idx == 0 else []))
                sop(lambda e: e.activation(aspc[:, 0:1], aspc[:, 0:1],
                                           AF.Copy, bias=0.0))
                slot_rd[sl] = c_v.at()
                act_rd[sl] = c_s.at()
                dma("gpsimd",
                    Tdst[a * 128:(a + 1) * 128, cb * 2048:(cb + 1) * 2048],
                    qout[sl][:], f_tw[sl], waits=[act_rd[sl]])
                out_wr[sl] = f_tw[sl].at()
                idx += 1
    quant_done_sp = [f_tw[0].at(), f_tw[1].at()]
    if "tg" in dbg:
        dma("gpsimd", dbg_out["tg"][:], TgS[:], f_st, waits=quant_done_sp)
    sb.free("qout1", "qout0", "aspc", "gtmp", "gacc",
            "gbuf1", "gbuf0")


    def proj_feature_major(wT_d, ndcb, bcol_d, src_d, ntot, dst, dbg_key):
        CCH = 256
        w_sb = sb.alloc("pf_w", [128, 16, 128 * ndcb], BF16)
        b_sb = sb.alloc("pf_b", [128, ndcb], F32)
        htb = [sb.alloc(f"pf_ht{i}", [128, 16, CCH], BF16) for i in range(2)]
        f_pfw = flow(f"f_pfw_{dbg_key}")
        f_pfb = flow(f"f_pfb_{dbg_key}")
        f_pfh = [flow(f"f_pfh0_{dbg_key}"), flow(f"f_pfh1_{dbg_key}")]
        dma("gpsimd", w_sb[:], wT_d[:].rearrange("(a p) n -> p a n", p=128),
            f_pfw, waits=fence())
        dma("sync", b_sb[:], bcol_d[:], f_pfb, waits=fence())
        wm = f_pfw.at()
        bm = f_pfb.at()
        slot_rd = [None, None]
        for cc in range(ntot // CCH):
            sl = cc % 2
            dma("gpsimd", htb[sl][:],
                src_d[:, cc * CCH:(cc + 1) * CCH].rearrange(
                    "(a p) t -> p a t", p=128), f_pfh[sl],
                waits=[slot_rd[sl]])
            lm = f_pfh[sl].at()
            for dcb in range(ndcb):
                bk = pbank[dcb % 2]
                for a in range(16):
                    mw = [lm, wm, bank_rd.get(id(bk))] if a == 0 else []
                    mm(bk[:, 0:CCH], w_sb[:, a, dcb * 128:(dcb + 1) * 128],
                       htb[sl][:, a, :], start=(a == 0), stop=(a == 15),
                       waits=mw, inc=(c_t if a == 15 else None))
                tm = c_t.at()
                vop(lambda e, bk=bk, dcb=dcb, cc=cc: e.tensor_scalar(
                    dst[:, dcb, cc * CCH:(cc + 1) * CCH], bk[:, 0:CCH],
                    b_sb[:, dcb:dcb + 1], None, OP.add), waits=[tm, bm])
                bank_rd[id(bk)] = c_v.at()
            slot_rd[sl] = c_t.at()
        if dbg_key in dbg:
            for dcb in range(ndcb):
                dma("gpsimd", dbg_out[dbg_key][dcb * 128:(dcb + 1) * 128, :],
                    dst[:, dcb, :], f_st, waits=[c_v.at()])
        sb.free("pf_ht1", "pf_ht0", "pf_b", "pf_w")

    proj_feature_major(wdkvT, DC // 128, bdkv_col, hT, S, ckvT, "ckv")

    def proj_token_major_kv(wT_d, b_bc_d, DSTs, dbg_key):
        w_sb = sb.alloc("kv_w", [128, DC // 128, D], BF16)
        b_sb = sb.alloc("kv_b", [128, D], F32)
        out_t = sb.alloc("kv_t", [128, D], BF16)
        f_kvw = flow(f"f_kvw_{dbg_key}")
        f_kvb = flow(f"f_kvb_{dbg_key}")
        dma("gpsimd", w_sb[:], wT_d[:].rearrange("(a p) n -> p a n", p=128),
            f_kvw, waits=fence())
        dma("sync", b_sb[:], b_bc_d[:], f_kvb, waits=fence())
        wm = f_kvw.at()
        bm = f_kvb.at()
        spill_last = None
        for st in range(ST):
            for hb in range(4):
                bk = pbank[hb % 4]
                for dcb in range(DC // 128):
                    mw = [wm, bank_rd.get(id(bk))] if dcb == 0 else []
                    mm(bk[:], ckvT[:, dcb, st * 128:(st + 1) * 128],
                       w_sb[:, dcb, hb * 512:(hb + 1) * 512],
                       start=(dcb == 0), stop=(dcb == DC // 128 - 1),
                       waits=mw, inc=(c_t if dcb == DC // 128 - 1 else None))
                tm = c_t.at()
                vop(lambda e, bk=bk, hb=hb: e.tensor_tensor(
                    out_t[:, hb * 512:(hb + 1) * 512], bk[:],
                    b_sb[:, hb * 512:(hb + 1) * 512], OP.add),
                    waits=[tm, bm, spill_last])
                bank_rd[id(bk)] = c_v.at()
            dma("sync", DSTs[st * 128:(st + 1) * 128, :], out_t[:], f_sp,
                waits=[c_v.at()])
            spill_last = f_sp.at()
        if dbg_key in dbg:
            dma("gpsimd", dbg_out[dbg_key][:], DSTs[:], f_st,
                waits=[f_sp.at()])
        sb.free("kv_t", "kv_b", "kv_w")
        return f_sp.at()

    proj_token_major_kv(wukT, buk_bc, KCs, "kc")
    kv_spill_mark = proj_token_major_kv(wuvT, buv_bc, Vs, "v")

    cqT = sb.alloc("cqT", [128, DQC // 128, NTOK], BF16)
    proj_feature_major(wdqT, DQC // 128, bdq_col, hTown, NTOK, cqT, "cq")

    def rope_tile(dst, src, cbuf, sbuf_, tmp, ncols, waits):
        n2 = 2 * ncols
        vop(lambda e: e.tensor_mul(dst[:, 0:n2:2], src[:, 0:n2:2], cbuf[:]),
            waits=waits)
        vop(lambda e: e.tensor_mul(tmp[:], src[:, 1:n2:2], sbuf_[:]))
        vop(lambda e: e.tensor_sub(dst[:, 0:n2:2], dst[:, 0:n2:2], tmp[:]))
        vop(lambda e: e.tensor_mul(dst[:, 1:n2:2], src[:, 0:n2:2], sbuf_[:]))
        vop(lambda e: e.tensor_mul(tmp[:], src[:, 1:n2:2], cbuf[:]))
        vop(lambda e: e.tensor_add(dst[:, 1:n2:2], dst[:, 1:n2:2], tmp[:]))

    psT_rd = [None, None]

    def transpose_block(dst_fn, src, nchunks, waits):
        for a in range(nchunks):
            tb = psT[a % 2]
            transp(tb[:, 0:128], src[:, a * 128:(a + 1) * 128],
                   waits=list(waits) + [psT_rd[a % 2]], inc=c_t)
            tm = c_t.at()
            vop(lambda e, tb=tb, a=a: e.tensor_copy(dst_fn(a), tb[:, 0:128]),
                waits=[tm])
            psT_rd[a % 2] = c_v.at()
        return c_v.at()

    wuq_sb = sb.alloc("wuq_sb", [128, DQC // 128, D], BF16)
    buq_sb = sb.alloc("buq_sb", [128, D], F32)
    qc_t = sb.alloc("qc_t", [128, D], BF16)
    qr_t = sb.alloc("qr_t", [128, D], BF16)
    rtmp = sb.alloc("rtmp", [128, 1024], BF16)
    csb = [sb.alloc(f"csb{i}", [128, 1024], BF16) for i in range(2)]
    ssb = [sb.alloc(f"ssb{i}", [128, 1024], BF16) for i in range(2)]
    f_q3w = flow("f_q3w")
    f_q3b = flow("f_q3b")
    f_cs = [flow("f_cs0"), flow("f_cs1")]
    dma("gpsimd", wuq_sb[:], wuqT[:].rearrange("(a p) n -> p a n", p=128),
        f_q3w, waits=fence())
    dma("sync", buq_sb[:], buq_bc[:], f_q3b, waits=fence())
    wm = f_q3w.at()
    bm = f_q3b.at()
    cs_rd = [None, None]
    qtr_prev = None
    qdbg_prev = None
    for ot in range(NT):
        sl = ot % 2
        dma("sync", csb[sl][:], cos_own[ot * 128:(ot + 1) * 128, :],
            f_cs[sl], waits=[cs_rd[sl]])
        dma("sync", ssb[sl][:], sin_own[ot * 128:(ot + 1) * 128, :],
            f_cs[sl])
        csm = f_cs[sl].at()
        for hb in range(4):
            bk = pbank[hb % 2]
            for qcb in range(DQC // 128):
                mw = [wm, bank_rd.get(id(bk))] if qcb == 0 else []
                mm(bk[:], cqT[:, qcb, ot * 128:(ot + 1) * 128],
                   wuq_sb[:, qcb, hb * 512:(hb + 1) * 512],
                   start=(qcb == 0), stop=(qcb == DQC // 128 - 1),
                   waits=mw, inc=(c_t if qcb == DQC // 128 - 1 else None))
            tm = c_t.at()
            vop(lambda e, bk=bk, hb=hb: e.tensor_tensor(
                qc_t[:, hb * 512:(hb + 1) * 512], bk[:],
                buq_sb[:, hb * 512:(hb + 1) * 512], OP.add),
                waits=[tm, bm, qtr_prev, qdbg_prev])
            bank_rd[id(bk)] = c_v.at()
        qc_mark = c_v.at()
        rope_tile(qr_t, qc_t, csb[sl], ssb[sl], rtmp, 1024,
                  [qc_mark, csm, qtr_prev, qdbg_prev])
        rope_mark = c_v.at()
        cs_rd[sl] = rope_mark
        transpose_block(lambda a, ot=ot: qCT[:, a, ot * 128:(ot + 1) * 128],
                        qc_t, 16, [qc_mark, ld0])
        transpose_block(lambda a, ot=ot: qRT[:, a, ot * 128:(ot + 1) * 128],
                        qr_t, 16, [rope_mark])
        qtr_prev = c_t.at()
        if "qc" in dbg:
            dma("gpsimd", dbg_out["qc"][ot * 128:(ot + 1) * 128, :], qc_t[:],
                f_st, waits=[qc_mark])
        if "qr" in dbg:
            dma("gpsimd", dbg_out["qr"][ot * 128:(ot + 1) * 128, :], qr_t[:],
                f_st, waits=[rope_mark])
        if "qc" in dbg or "qr" in dbg:
            qdbg_prev = f_st.at()
    sb.free("ssb1", "ssb0", "csb1", "csb0", "rtmp", "qr_t", "qc_t",
            "buq_sb", "wuq_sb", "cqT")

    wkr_sb = sb.alloc("wkr_sb", [128, 16, 1024], BF16)
    bkr_sb = sb.alloc("bkr_sb", [128, D], F32)
    kr_t = sb.alloc("kr_t", [128, 1024], BF16)
    krr_t = sb.alloc("krr_t", [128, 1024], BF16)
    rtmp = sb.alloc("rtmp", [128, 512], BF16)
    csb = [sb.alloc(f"csb{i}", [128, 512], BF16) for i in range(2)]
    ssb = [sb.alloc(f"ssb{i}", [128, 512], BF16) for i in range(2)]
    htb = [sb.alloc(f"htb{i}", [128, 16, 128], BF16) for i in range(2)]
    f_p4w = flow("f_p4w")
    f_p4b = flow("f_p4b")
    f_p4h = [flow("f_p4h0"), flow("f_p4h1")]
    f_p4c = [flow("f_p4c0"), flow("f_p4c1")]
    dma("sync", bkr_sb[:], bkr_bc[:], f_p4b, waits=fence())
    bm = f_p4b.at()
    for half in range(2):
        dma("gpsimd", wkr_sb[:],
            wkrT[:, half * 1024:(half + 1) * 1024].rearrange(
                "(a p) n -> p a n", p=128), f_p4w, waits=fence())
        wm = f_p4w.at()
        slot_rd = [None, None]
        cs_rd = [None, None]
        spill_last = None
        for st in range(ST):
            sl = st % 2
            dma("gpsimd", htb[sl][:],
                hT[:, st * 128:(st + 1) * 128].rearrange(
                    "(a p) t -> p a t", p=128), f_p4h[sl],
                waits=[slot_rd[sl]])
            lm = f_p4h[sl].at()
            dma("sync", csb[sl][:],
                cosT[st * 128:(st + 1) * 128, half * 512:(half + 1) * 512],
                f_p4c[sl], waits=[cs_rd[sl]])
            dma("sync", ssb[sl][:],
                sinT[st * 128:(st + 1) * 128, half * 512:(half + 1) * 512],
                f_p4c[sl])
            csm = f_p4c[sl].at()
            for hb in range(2):
                bk = pbank[hb % 2]
                for a in range(16):
                    mw = [lm, wm, bank_rd.get(id(bk))] if a == 0 else []
                    mm(bk[:], htb[sl][:, a, :],
                       wkr_sb[:, a, hb * 512:(hb + 1) * 512],
                       start=(a == 0), stop=(a == 15),
                       waits=mw, inc=(c_t if a == 15 else None))
                tm = c_t.at()
                vop(lambda e, bk=bk, hb=hb, half=half: e.tensor_tensor(
                    kr_t[:, hb * 512:(hb + 1) * 512], bk[:],
                    bkr_sb[:, half * 1024 + hb * 512:
                           half * 1024 + (hb + 1) * 512], OP.add),
                    waits=[tm, bm])
                bank_rd[id(bk)] = c_v.at()
            kr_mark = c_v.at()
            rope_tile(krr_t, kr_t, csb[sl], ssb[sl], rtmp, 512,
                      [kr_mark, csm, spill_last])
            rope_mark = c_v.at()
            cs_rd[sl] = rope_mark
            slot_rd[sl] = c_t.at()
            dma("sync", KRs[st * 128:(st + 1) * 128,
                            half * 1024:(half + 1) * 1024], krr_t[:], f_sp,
                waits=[rope_mark])
            spill_last = f_sp.at()
    kr_spill_mark = f_sp.at()
    if "kr" in dbg:
        dma("gpsimd", dbg_out["kr"][:], KRs[:], f_st, waits=[kr_spill_mark])
    sb.free("htb1", "htb0", "ssb1", "ssb0", "csb1", "csb0", "rtmp",
            "krr_t", "kr_t", "bkr_sb", "wkr_sb")

    arena_f = sb.alloc("arena_f", [128, 16512], F32)
    O_acc = arena_f[:, 0:16384].rearrange("p (a d) -> p a d", a=NT)
    lacc = arena_f[:, 16384:16512].rearrange("p (a h) -> p a h", a=NT)
    x1g = arena_f[:, 0:8192].rearrange("p (a d) -> p a d", a=4)
    r2g = arena_f[:, 8192:16384].rearrange("p (a d) -> p a d", a=4)
    kctb = [sb.alloc(f"kctb{i}", [128, 8, CH], BF16) for i in range(2)]
    krtb = [sb.alloc(f"krtb{i}", [128, 8, CH], BF16) for i in range(2)]
    vtb_b = sb.alloc("vtb", [128, 4, 1024], BF16)
    vtb = [vtb_b, vtb_b]
    P_sb = [sb.alloc(f"P_sb{i}", [128, CH], BF16) for i in range(2)]
    PT_sb = [sb.alloc(f"PT_sb{i}", [128, CH], BF16) for i in range(2)]
    ltmp = sb.alloc("ltmp", [128, 2], F32)
    f_at = [flow("f_at0"), flow("f_at1")]

    fm = fence()
    unit = 0
    psS = [pbank[0], pbank[1]]
    psAV = [pbank[2], pbank[3]]
    exp_done = {}
    p_free = [None, None]
    l_free = [None, None]
    pt_free = [None, None]
    av_evac = {}
    ld_consumed = [None, None]
    for c in range(NCH):
        for hg in range(2):
            par = (2 * c + hg) % 2
            for h8 in range(8):
                h = hg * 8 + h8
                dmaT(kctb[par][:, h8, :],
                     KCs[c * CH:(c + 1) * CH, h * 128:(h + 1) * 128],
                     f_at[par], waits=[ld_consumed[par], kv_spill_mark,
                                       kr_spill_mark] + (fm if c == 0 else []))
                dmaT(krtb[par][:, h8, :],
                     KRs[c * CH:(c + 1) * CH, h * 128:(h + 1) * 128],
                     f_at[par])
            dma("sync", vtb[par][:],
                Vs[c * CH:(c + 1) * CH,
                   hg * 1024:(hg + 1) * 1024].rearrange(
                       "(s p) d -> p s d", p=128), f_at[par],
                waits=[ld_consumed[1 - par]])
            ldm = f_at[par].at()
            for h8 in range(8):
                h = hg * 8 + h8
                for t in range(c, NT):
                    us = unit % 2
                    sbk = psS[us]
                    mm(sbk[:], qCT[:, h, t * 128:(t + 1) * 128],
                       kctb[par][:, h8, :], start=True, stop=False,
                       waits=[ldm, exp_done.get(id(sbk))])
                    mm(sbk[:], qRT[:, h, t * 128:(t + 1) * 128],
                       krtb[par][:, h8, :], start=False, stop=True, inc=c_t)
                    sm = c_t.at()
                    sop(lambda e, us=us, sbk=sbk: e.activation(
                        P_sb[us][:], sbk[:], AF.Exp, scale=SM_SCALE,
                        accum_out=ltmp[:, us:us + 1]),
                        waits=[sm, p_free[us], l_free[us]])
                    exp_done[id(sbk)] = c_s.at()
                    em = c_s.at()
                    lw = [em]
                    if t == c:
                        vop(lambda e, us=us: e.tensor_mul(
                            P_sb[us][:], P_sb[us][:], mask_sb[:]),
                            waits=[em, ld0])
                        vop(lambda e, us=us: e.reduce_sum(
                            ltmp[:, us:us + 1], P_sb[us][:], axis=AX))
                        lw = [c_v.at()]
                    lvm = c_v.at()
                    tb = psT[us]
                    for s2 in range(4):
                        transp(tb[:, s2 * 128:(s2 + 1) * 128],
                               P_sb[us][:, s2 * 128:(s2 + 1) * 128],
                               waits=([em, lvm, psT_rd[us]] if s2 == 0
                                      else []),
                               inc=(c_t if s2 == 3 else None))
                    trm = c_t.at()
                    p_free[us] = trm
                    vop(lambda e, us=us, tb=tb: e.tensor_copy(
                        PT_sb[us][:], tb[:, 0:512]),
                        waits=[trm, pt_free[us]])
                    psT_rd[us] = c_v.at()
                    ptm = c_v.at()
                    if c == 0:
                        vop(lambda e, t=t, h=h, us=us: e.tensor_copy(
                            lacc[:, t, h:h + 1], ltmp[:, us:us + 1]),
                            waits=lw)
                    else:
                        vop(lambda e, t=t, h=h, us=us: e.tensor_add(
                            lacc[:, t, h:h + 1], lacc[:, t, h:h + 1],
                            ltmp[:, us:us + 1]), waits=lw)
                    l_free[us] = c_v.at()
                    av = psAV[us]
                    for s2 in range(4):
                        mw = [ptm, av_evac.get(id(av))] if s2 == 0 else []
                        mm(av[:, 0:128],
                           PT_sb[us][:, s2 * 128:(s2 + 1) * 128],
                           vtb[par][:, s2, h8 * 128:(h8 + 1) * 128],
                           start=(s2 == 0), stop=(s2 == 3),
                           waits=mw, inc=(c_t if s2 == 3 else None))
                    avm = c_t.at()
                    pt_free[us] = avm
                    if c == 0:
                        vop(lambda e, t=t, h=h, av=av: e.tensor_copy(
                            O_acc[:, t, h * 128:(h + 1) * 128], av[:, 0:128]),
                            waits=[avm])
                    else:
                        vop(lambda e, t=t, h=h, av=av: e.tensor_add(
                            O_acc[:, t, h * 128:(h + 1) * 128],
                            O_acc[:, t, h * 128:(h + 1) * 128], av[:, 0:128]),
                            waits=[avm])
                    av_evac[id(av)] = c_v.at()
                    unit += 1
            ld_consumed[par] = c_t.at()
    attn_done = fence()
    if "lacc" in dbg:
        dma("sync", dbg_out["lacc"][:],
            lacc[:].rearrange("p a h -> p (a h)"), f_st, waits=attn_done)
    if "oacc" in dbg:
        for t in range(NT):
            dma("sync", dbg_out["oacc"][t * 128:(t + 1) * 128, :],
                O_acc[:, t, :], f_st, waits=attn_done)
    sb.free("ltmp", "PT_sb1", "PT_sb0", "P_sb1", "P_sb0",
            "vtb", "krtb1", "krtb0", "kctb1", "kctb0")

    sqs = sb.alloc("sqs", [128, D // 2], F32)
    bo_sb = sb.alloc("bo_sb", [128, D], F32)
    hob = [sb.alloc(f"hob{i}", [128, D], F32) for i in range(2)]
    o_t = sb.alloc("o_t", [128, D], BF16)
    oT_sb = sb.alloc("oT_sb", [128, 16, 128], BF16)
    x1_t = sb.alloc("x1_t", [128, D], F32)
    linv = sb.alloc("linv", [128, H], F32)
    f_wo = flow("f_wo")
    f_ob = flow("f_ob")
    f_ho = [flow("f_ho0"), flow("f_ho1")]
    if "nowoload" not in ablate:
        dma("gpsimd", wo_view[:],
            woT[:].rearrange("(a p) n -> p a n", p=128),
            f_wo, waits=attn_done)
    dma("sync", bo_sb[:], bo_bc[:], f_ob, waits=attn_done)
    wm = f_wo.at()
    bm = f_ob.at()
    ho_rd = [None, None]
    ln_state = {"sqs_rd": None}
    if "lnx" in dbg:
        lnsave = sb.alloc("lnsave", [128, NT, 8], F32)

    def layernorm(x_ap, r_mark, ln_out=None, scratch=None):
        sq = scratch if scratch is not None else sqs
        rm = r_mark if isinstance(r_mark, list) else [r_mark]
        h_ = D // 2
        vop(lambda e: e.tensor_mul(sq[:], x_ap[:, 0:h_], x_ap[:, 0:h_]),
            waits=rm)
        vop(lambda e: e.reduce_sum(lnc[:, 0:1], x_ap, axis=AX))
        vop(lambda e: e.reduce_sum(lnc[:, 2:3], sq[:], axis=AX))
        vop(lambda e: e.tensor_mul(sq[:], x_ap[:, h_:D], x_ap[:, h_:D]))
        vop(lambda e: e.reduce_sum(lnc[:, 6:7], sq[:], axis=AX))
        vop(lambda e: e.tensor_scalar(lnc[:, 1:2], lnc[:, 0:1], 1.0 / D,
                                      None, OP.mult))
        vop(lambda e: e.tensor_copy(epsc[:], epsc[:]))
        vop(lambda e: e.tensor_add(lnc[:, 2:3], lnc[:, 2:3], lnc[:, 6:7]))
        vop(lambda e: e.tensor_copy(epsc[:], epsc[:]))
        vop(lambda e: e.tensor_scalar(lnc[:, 3:4], lnc[:, 2:3], 1.0 / D,
                                      None, OP.mult))
        vop(lambda e: e.tensor_mul(lnc[:, 4:5], lnc[:, 1:2], lnc[:, 1:2]))
        vop(lambda e: e.tensor_copy(epsc[:], epsc[:]))
        vop(lambda e: e.tensor_sub(lnc[:, 3:4], lnc[:, 3:4], lnc[:, 4:5]))
        vop(lambda e: e.tensor_copy(epsc[:], epsc[:]))
        sop(lambda e: e.activation(lnc[:, 4:5], lnc[:, 3:4], AF.Sqrt,
                                   bias=epsc[:, 0:1]),
            waits=[c_v.at(), init_mark])
        vop(lambda e: e.tensor_copy(lnc[:, 6:7], lnc[:, 6:7]),
            waits=[c_s.at()])
        vop(lambda e: e.reciprocal(lnc[:, 5:6], lnc[:, 4:5]))
        vop(lambda e: e.tensor_copy(lnc[:, 7:8], lnc[:, 6:7]))
        vop(lambda e: e.tensor_scalar(ln_out if ln_out is not None else x_ap,
                                      x_ap, lnc[:, 1:2], lnc[:, 5:6],
                                      OP.subtract, OP.mult))
        return c_v.at()

    if "can" in dbg:
        can_words = (nc.sbuf_top - nc.sbuf_base) // 4 - 64
        canary = sb.alloc("canary", [128, can_words], F32)
        dbg_out["can"] = nc.dram_tensor("dbg_can", [128, can_words], F32,
                                        kind="ExternalOutput").ap()
        dbg_out["canbase"] = nc.dram_tensor("dbg_canbase", [1, 1], F32,
                                            kind="ExternalOutput").ap()
        can_base_addr = nc.sbuf_base - can_words * 4
        dbg_out["can2"] = nc.dram_tensor("dbg_can2", [128, can_words], F32,
                                         kind="ExternalOutput").ap()
        vop(lambda e: e.memset(canary[:], 1234.5), waits=fence())
        can_set = c_v.at()
        dma("sync", dbg_out["can2"][:], canary[:], f_st, waits=[can_set])
    otr_prev = None
    x1sp_prev = None
    x1dbg_prev = None
    tile_fence = []
    for t in range(NT):
        sl = t % 2
        dma("sync", hob[sl][:], h_own[t * 128:(t + 1) * 128, :], f_ho[sl],
            waits=[ho_rd[sl]] + (attn_done if t == 0 else []))
        hm = f_ho[sl].at()
        if "nowo" in ablate:
            for db in range(4):
                vop(lambda e, db=db, sl=sl: e.tensor_tensor(
                    x1_t[:, db * 512:(db + 1) * 512],
                    bo_sb[:, db * 512:(db + 1) * 512],
                    hob[sl][:, db * 512:(db + 1) * 512], OP.add),
                    waits=[hm, bm, x1sp_prev] + (attn_done if t == 0 else []))
            ho_rd[sl] = c_v.at()
            r_marks = [c_v.at()]
            x1m = layernorm(x1_t[:], r_marks) if "noln" not in ablate \
                else r_marks[0]
            dma("sync", X1s[t * 128:(t + 1) * 128, :], x1_t[:], f_sp,
                waits=[x1m])
            x1sp_prev = f_sp.at()
            if "can" in dbg:
                key = f"can3_{t}"
                dbg_out[key] = nc.dram_tensor("dbg_" + key,
                                              [128, canary.shape[1]], F32,
                                              kind="ExternalOutput").ap()
                dma("sync", dbg_out[key][:], canary[:], f_st,
                    waits=[x1sp_prev, can_set])
            continue
        vop(lambda e, t=t: e.reciprocal(linv[:], lacc[:, t, :]),
            waits=(attn_done if t == 0 else []) +
                  (tile_fence if "serial7b" in ablate else []))
        vop(lambda e: e.tensor_copy(lnc[:, 6:7], lnc[:, 6:7]))
        for h in range(H):
            vop(lambda e, t=t, h=h: e.tensor_scalar(
                o_t[:, h * 128:(h + 1) * 128],
                O_acc[:, t, h * 128:(h + 1) * 128],
                linv[:, h:h + 1], None, OP.mult),
                waits=([otr_prev] if h == 0 else []))
        om = c_v.at()
        if "o" in dbg:
            dma("gpsimd", dbg_out["o"][t * 128:(t + 1) * 128, :], o_t[:],
                f_st, waits=[om])
        transpose_block(lambda a: oT_sb[:, a, :], o_t, 16,
                        [om] + ([f_st.at()] if "o" in dbg else []))
        otr_prev = c_t.at()
        otm = c_v.at()
        if "ot" in dbg:
            dma("gpsimd", dbg_out["ot"][t * 128:(t + 1) * 128, :],
                oT_sb[:].rearrange("p a t -> p (a t)"), f_st, waits=[otm])
            otm_l = [otm, f_st.at()]
        else:
            otm_l = [otm]
        for db in range(4):
            bk = pbank[db]
            for a in range(16):
                mw = (otm_l + [wm, bank_rd.get(id(bk)), av_evac.get(id(bk))]
                      + attn_done) if a == 0 else []
                mm(bk[:], oT_sb[:, a, :],
                   wo_view[:, a, db * 512:(db + 1) * 512],
                   start=(a == 0), stop=(a == 15),
                   waits=mw, inc=(c_t if a == 15 else None))
            tm = c_t.at()
            vop(lambda e, bk=bk, db=db: e.tensor_tensor(
                x1_t[:, db * 512:(db + 1) * 512], bk[:],
                bo_sb[:, db * 512:(db + 1) * 512], OP.add),
                waits=[tm, bm, x1sp_prev, x1dbg_prev])
            if "noh" not in ablate:
                vop(lambda e, db=db, sl=sl: e.tensor_add(
                    x1_t[:, db * 512:(db + 1) * 512],
                    x1_t[:, db * 512:(db + 1) * 512],
                    hob[sl][:, db * 512:(db + 1) * 512]), waits=[hm])
            bank_rd[id(bk)] = c_v.at()
        ho_rd[sl] = c_v.at()
        r_marks = [c_v.at()]
        if "r1" in dbg:
            dma("gpsimd", dbg_out["r1"][t * 128:(t + 1) * 128, :], x1_t[:],
                f_st, waits=[c_v.at()])
            r_marks.append(f_st.at())
        if "noln" in ablate:
            x1m = r_marks[0]
        elif "lnsq" in ablate:
            sop(lambda e: e.activation(sqs[:], x1_t[:], AF.Square),
                waits=r_marks)
            rec.emit("vector", lambda e: e.nop(), waits=[c_s.at()], inc=c_v)
            x1m = c_v.at()
        elif "lnsq2" in ablate:
            sop(lambda e: e.activation(x1o_t[:], x1_t[:], AF.Square),
                waits=r_marks)
            rec.emit("vector", lambda e: e.nop(), waits=[c_s.at()], inc=c_v)
            x1m = c_v.at()
        elif "lnsq3" in ablate:
            sop(lambda e: e.activation(sqs[:, 0:512], x1_t[:, 0:512],
                                       AF.Square), waits=r_marks)
            rec.emit("vector", lambda e: e.nop(), waits=[c_s.at()], inc=c_v)
            x1m = c_v.at()
        elif "lnrd" in ablate:
            vop(lambda e: e.reduce_sum(lnc[:, 0:1], x1_t[:], axis=AX),
                waits=r_marks)
            vop(lambda e: e.tensor_copy(lnc[:, 6:7], lnc[:, 6:7]))
            x1m = c_v.at()
        else:
            x1m = layernorm(x1_t[:], r_marks)
            if "lnx" in dbg:
                vop(lambda e, t=t: e.tensor_copy(lnsave[:, t, :], lnc[:]))
        dma("sync", X1s[t * 128:(t + 1) * 128, :], x1_t[:], f_sp,
            waits=[x1m])
        x1sp_prev = f_sp.at()
        if "can" in dbg and t == 0:
            dbg_out["can3"] = nc.dram_tensor("dbg_can3", canary.shape[1:2]
                                             and [128, canary.shape[1]], F32,
                                             kind="ExternalOutput").ap()
            dma("sync", dbg_out["can3"][:], canary[:], f_st,
                waits=[x1sp_prev, can_set])
        tile_fence = fence() + [x1sp_prev]
        if "x1" in dbg:
            dma("gpsimd", dbg_out["x1"][t * 128:(t + 1) * 128, :], x1_t[:],
                f_st, waits=[x1m])
            x1dbg_prev = f_st.at()
    x1_all = f_sp.at()
    if "wod" in dbg:
        dma("gpsimd", dbg_out["wod"][:], arena_b[:], f_st, waits=fence())
    if "lnx" in dbg:
        dma("sync", dbg_out["lnx"][:],
            lnsave[:].rearrange("p a b -> p (a b)"), f_st, waits=[c_v.at()])
    if "x1s" in dbg:
        dma("gpsimd", dbg_out["x1s"][:], X1s[:], f_st, waits=[x1_all])
    wo_done = fence()
    if "can" in dbg:
        sb.free("canary")
    if "lnx" in dbg:
        sb.free("lnsave")
    sb.free("linv", "x1_t", "oT_sb", "o_t", "hob1", "hob0",
            "bo_sb", "sqs")

    for k, shp, dt in (("xqt", [128, 8192], BF16), ("h3r", [128, 32768], BF16),
                       ("h3q", [128, 32768], BF16), ("r2", [128, 8192], F32),
                       ("sr8", [128, 24], F32)):
        if k in dbg:
            dbg_out[k] = nc.dram_tensor("dbg_" + k, shp,
                                        dt if k not in ("r2", "sr8") else F32,
                                        kind="ExternalOutput").ap()
    srow = sb.alloc("srow", [128, 4, 6], F32)
    g3c = sb.alloc("g3c", [128, 4, 2], F32)
    s3xb = sb.alloc("s3xb", [128, 128], F32)
    s3col = sb.alloc("s3col", [128, 128], BF16)
    sqs8 = sb.alloc("sqs8", [128, D // 2], F32)
    vop(lambda e: e.memset(s3col[:], 0.0), waits=wo_done)

    for g2 in range(2 if not no_ffn else 0):
        xqT = sb.alloc("xqT", [128, 4, 16, 128], BF16)
        xq_t = sb.alloc("xq_t", [128, D], BF16)
        h3b = sb.alloc("h3b", [128, CH], BF16)
        gv1 = sb.alloc("gv1", [128, CH], F32)
        gv2 = sb.alloc("gv2", [128, CH], F32)
        gv3 = sb.alloc("gv3", [128, CH], F32)
        gsave = sb.alloc("gsave", [128, 4, CH], BF16)
        tsl = sb.alloc("tsl", [128, 2, 4, CH], BF16)
        bgb = sb.alloc("bgb", [128, CH], F32)
        bvb = sb.alloc("bvb", [128, CH], F32)
        f_x1g = flow(f"f_x1g_{g2}")
        f_slab = flow(f"f_slab_{g2}")
        f_tob = flow(f"f_tob_{g2}")

        dma("sync", x1g[:],
            X1s[g2 * 512:(g2 + 1) * 512, :].rearrange(
                "(a p) d -> p a d", p=128),
            f_x1g, waits=[x1_all] + (wo_done if g2 == 0 else fence()))
        x1lm = f_x1g.at()
        xq_free = None
        for i in range(4):
            vop(lambda e, i=i: e.tensor_reduce(
                srow[:, i, 0:1], x1g[:, i, :], axis=AX, op=OP.max,
                apply_absolute_value=True), waits=[x1lm])
        for i in range(4):
            vop(lambda e, i=i: e.tensor_scalar(
                srow[:, i, 0:1], srow[:, i, 0:1], 1e-5, None, OP.max))
        for i in range(4):
            vop(lambda e, i=i: e.reciprocal(srow[:, i, 1:2],
                                            srow[:, i, 0:1]))
        for i in range(4):
            vop(lambda e, i=i: e.tensor_scalar(
                srow[:, i, 1:2], srow[:, i, 1:2], 127.0, None, OP.mult))
        for i in range(4):
            vop(lambda e, i=i: e.tensor_mul(srow[:, i, 2:3],
                                            srow[:, i, 0:1], gam[:, 6:7]),
                waits=[gam_mark])
            vop(lambda e, i=i: e.tensor_mul(srow[:, i, 3:4],
                                            srow[:, i, 0:1], gam[:, 7:8]))
        for i in range(4):
            for hf in range(2):
                vop(lambda e, i=i, hf=hf: e.tensor_scalar(
                    sqs8[:], x1g[:, i, hf * 1024:(hf + 1) * 1024],
                    srow[:, i, 1:2], MAGIC, OP.mult, OP.add))
                vop(lambda e, hf=hf: e.tensor_scalar(
                    xq_t[:, hf * 1024:(hf + 1) * 1024], sqs8[:], MAGIC, None,
                    OP.subtract), waits=[xq_free] if hf == 0 else [])
            xqm = c_v.at()
            transpose_block(lambda a, i=i: xqT[:, i, a, :], xq_t, 16, [xqm])
            xq_free = c_t.at()
        xq_mark = c_v.at()
        if "xqt" in dbg and g2 == 0:
            dma("gpsimd", dbg_out["xqt"][:],
                xqT[:].rearrange("p i a t -> p (i a t)"), f_st,
                waits=[xq_mark, xq_free])

        slab_rd = None
        silu_prev = None
        h3b_free = None
        slab_ring = [0]
        slab_free = {0: None, 1: None}
        for hb in range(HID // CH):
            dma("sync", bgb[:], bg_bc[:, hb * CH:(hb + 1) * CH], f_slab,
                waits=[slab_rd] + quant_done_sp)
            dma("sync", bvb[:], bv_bc[:, hb * CH:(hb + 1) * CH], f_slab)
            for half, TS_ in ((0, TgS), (1, TgS), (2, TvS), (3, TvS)):
                hh = half % 2
                for sub in range(2):
                    sl2 = slab_ring[0]
                    slab_ring[0] ^= 1
                    dma("sync", tsl[:, sl2, :, :],
                        TS_[hh * 1024 + sub * 512:
                            hh * 1024 + (sub + 1) * 512,
                            hb * CH:(hb + 1) * CH].rearrange(
                            "(a p) n -> p a n", p=128), f_slab,
                        waits=[slab_free[sl2]] +
                              ([x1_all] if (g2 == 0 and hb == 0 and
                                            half == 0 and sub == 0)
                               else []))
                    slm = f_slab.at()
                    for i in range(4):
                        bk = pbank[i]
                        for a2 in range(4):
                            a = hh * 8 + sub * 4 + a2
                            mw = ([slm, xq_mark, bank_rd.get(id(bk)),
                                   av_evac.get(id(bk))] if a2 == 0 else [])
                            mm(bk[:], xqT[:, i, a, :], tsl[:, sl2, a2, :],
                               start=(a == 0), stop=(a == 15),
                               waits=mw,
                               inc=(c_t if (a == 15 or
                                            (i == 3 and a2 == 3))
                                    else None))
                    slab_free[sl2] = c_t.at()
                if half == 1:
                    for i in range(4):
                        bk = pbank[i]
                        vop(lambda e, bk=bk, i=i: e.scalar_tensor_tensor(
                            gv1[:], bk[:], srow[:, i, 2:3], bgb[:],
                            OP.mult, OP.add),
                            waits=[c_t.at(), silu_prev])
                        bank_rd[id(bk)] = c_v.at()
                        sop(lambda e: e.activation(gv2[:], gv1[:], AF.Silu),
                            waits=[c_v.at()])
                        silu_prev = c_s.at()
                        vop(lambda e, i=i: e.tensor_copy(gsave[:, i, :],
                                                         gv2[:]),
                            waits=[silu_prev])
            for i in range(4):
                bk = pbank[i]
                vop(lambda e, bk=bk, i=i: e.scalar_tensor_tensor(
                    gv3[:], bk[:], srow[:, i, 3:4], bvb[:],
                    OP.mult, OP.add), waits=[c_t.at(), h3b_free])
                bank_rd[id(bk)] = c_v.at()
                vop(lambda e, i=i: e.tensor_mul(h3b[:], gsave[:, i, :],
                                                gv3[:]))
                vop(lambda e, i=i: e.tensor_reduce(
                    g3c[:, i, 1:2], h3b[:], axis=AX, op=OP.max,
                    apply_absolute_value=True))
                h3m = c_v.at()
                transpose_block(
                    lambda s2, i=i, hb=hb: h3T[:, i, hb * 4 + s2, :],
                    h3b, 4, [h3m] + (wo_done if (g2 == 0 and hb == 0)
                                     else []))
                h3b_free = c_t.at()
                if hb == 0:
                    vop(lambda e, i=i: e.tensor_copy(g3c[:, i, 0:1],
                                                     g3c[:, i, 1:2]))
                else:
                    vop(lambda e, i=i: e.tensor_max(
                        g3c[:, i, 0:1], g3c[:, i, 0:1], g3c[:, i, 1:2]))
            slab_rd = c_t.at()
        h3_mark = c_v.at()
        h3tap = []
        if "h3r" in dbg and g2 == 0:
            dma("gpsimd", dbg_out["h3r"][:],
                arena_b[:, 0:32768], f_st, waits=[h3_mark])
            h3tap = [f_st.at()]
        sb.free("bvb", "bgb", "tsl", "gsave", "gv3", "gv2", "gv1", "h3b",
                "xq_t", "xqT")

        qtmp = sb.alloc("qtmp", [128, 2, 128], F32)
        for i in range(4):
            vop(lambda e, i=i: e.tensor_scalar(
                g3c[:, i, 0:1], g3c[:, i, 0:1], 1e-5, None, OP.max),
                waits=[h3_mark])
        for i in range(4):
            vop(lambda e, i=i: e.reciprocal(g3c[:, i, 1:2], g3c[:, i, 0:1]))
            vop(lambda e, i=i: e.tensor_mul(srow[:, i, 2:3], g3c[:, i, 0:1],
                                            gam[:, 8:9]))
        for i in range(4):
            vop(lambda e, i=i: e.tensor_scalar(
                g3c[:, i, 1:2], g3c[:, i, 1:2], 127.0, None, OP.mult))
        for i in range(4):
            tb = psT[0]
            vop(lambda e, i=i: e.tensor_copy(s3col[:, 0:1], g3c[:, i, 1:2]))
            vop(lambda e: e.tensor_copy(epsc[:], epsc[:]))
            transp(tb[:, 0:128], s3col[:], waits=[c_v.at(), psT_rd[0]],
                   inc=c_t)
            vop(lambda e, tb=tb: e.tensor_copy(s3row[0:1, :],
                                               tb[0:1, 0:128]),
                waits=[c_t.at()])
            psT_rd[0] = c_v.at()
            mm(pbank[0][:, 0:128], onesrow_bf[:], s3row[0:1, :],
               start=True, stop=True,
               waits=[c_v.at(), bank_rd.get(id(pbank[0])),
                      av_evac.get(id(pbank[0]))], inc=c_t)
            vop(lambda e: e.tensor_copy(s3xb[:], pbank[0][:, 0:128]),
                waits=[c_t.at()])
            bank_rd[id(pbank[0])] = c_v.at()
            vop(lambda e: e.tensor_copy(lnc[:, 6:7], lnc[:, 6:7]))
            s3bm = c_v.at()
            for a in range(64):
                vop(lambda e, i=i, a=a: e.tensor_mul(
                    qtmp[:, a % 2, :], h3T[:, i, a, :], s3xb[:]),
                    waits=([s3bm] + h3tap) if a == 0 else [])
                if a > 0:
                    vop(lambda e, i=i, a=a: e.tensor_scalar(
                        h3T[:, i, a - 1, :], qtmp[:, (a - 1) % 2, :],
                        MAGIC, MAGIC, OP.add, OP.subtract))
            vop(lambda e: e.tensor_copy(epsc[:], epsc[:]))
            vop(lambda e, i=i: e.tensor_scalar(
                h3T[:, i, 63, :], qtmp[:, 1, :], MAGIC, MAGIC, OP.add,
                OP.subtract))
        h3q_mark = c_v.at()
        h3qtap = []
        if "h3q" in dbg and g2 == 0:
            dma("gpsimd", dbg_out["h3q"][:],
                arena_b[:, 0:32768], f_st, waits=[h3q_mark])
            h3qtap = [f_st.at()]
        if "sr8" in dbg and g2 == 0:
            dma("gpsimd", dbg_out["sr8"][:, 0:24],
                srow[:].rearrange("p i c -> p (i c)"), f_st,
                waits=[h3q_mark])
            h3qtap = h3qtap + [f_st.at()]
        sb.free("qtmp")

        DBW = 256
        tob = sb.alloc("tob", [128, 64, DBW], BF16)
        bob = sb.alloc("bob", [128, DBW], F32)
        to_rd = None
        for db in range(D // DBW):
            dma("sync", tob[:],
                ToS[:, db * DBW:(db + 1) * DBW].rearrange(
                    "(a p) n -> p a n", p=128),
                f_tob, waits=[to_rd] + quant_done_sp +
                             ([h3q_mark, c_t.at()] if db == 0 else []))
            dma("sync", bob[:], bout_bc[:, db * DBW:(db + 1) * DBW], f_tob)
            tolm = f_tob.at()
            for i in range(4):
                bk = pbank[i % 4]
                for a in range(64):
                    mw = ([tolm, h3q_mark, bank_rd.get(id(bk)),
                           av_evac.get(id(bk))] + h3qtap) if a == 0 else []
                    mm(bk[:, 0:DBW], h3T[:, i, a, :], tob[:, a, :],
                       start=(a == 0), stop=(a == 63),
                       waits=mw, inc=(c_t if a == 63 else None))
                tm = c_t.at()
                vop(lambda e, bk=bk, i=i, db=db: e.scalar_tensor_tensor(
                    r2g[:, i, db * DBW:(db + 1) * DBW], bk[:, 0:DBW],
                    srow[:, i, 2:3], bob[:], OP.mult, OP.add),
                    waits=[tm])
                bank_rd[id(bk)] = c_v.at()
            to_rd = c_t.at()
        r2tap = []
        if "r2" in dbg and g2 == 0:
            r2m = c_v.at()
            dma("gpsimd", dbg_out["r2"][:],
                r2g[:].rearrange("p i d -> p (i d)"), f_st, waits=[r2m])
            r2tap = [f_st.at()]
        for i in range(4):
            t = g2 * 4 + i
            vop(lambda e, i=i: e.tensor_add(r2g[:, i, :], r2g[:, i, :],
                                            x1g[:, i, :]),
                waits=[x1lm] + r2tap)
            om = layernorm(r2g[:, i, :], c_v.at(), scratch=sqs8)
            dma("sync", OUT[t * 128:(t + 1) * 128, :], r2g[:, i, :], f_st,
                waits=[om])
        sb.free("bob", "tob")

    final_waits = [fl.at() for fl in flows.values()]
    rec.emit("sync", lambda e: e.nop(), waits=final_waits)
    rec.emit("sync", lambda e: e.sem_clear(range(SEM_LO, SEM_HI)))

    with nc.Block() as block:
        rec.replay(block)
    ctx.close()
    return nc



def _rope_tables():
    idx = np.arange(DH // 2, dtype=np.float32)
    pos = np.arange(S, dtype=np.float32)
    freq = np.float32(10000.0) ** (np.float32(-2.0) * idx / np.float32(DH))
    theta = pos[:, None].astype(np.float32) * freq[None, :].astype(np.float32)
    cos = np.cos(theta.astype(np.float64)).astype(np.float32)
    sin = np.sin(theta.astype(np.float64)).astype(np.float32)
    cosT = np.ascontiguousarray(np.tile(cos, (1, H))).astype(BF)
    sinT = np.ascontiguousarray(np.tile(sin, (1, H))).astype(BF)
    return cosT, sinT


def kernel(
    L0_attn_h, L0_attn_W_DKV_w, L0_attn_W_DKV_b, L0_attn_W_UK_w,
    L0_attn_W_UK_b, L0_attn_W_UV_w, L0_attn_W_UV_b, L0_attn_W_DQ_w,
    L0_attn_W_DQ_b, L0_attn_W_UQ_w, L0_attn_W_UQ_b, L0_attn_W_KR_w,
    L0_attn_W_KR_b, L0_attn_W_O_w, L0_attn_W_O_b, L0_ffn_gate_w,
    L0_ffn_gate_b, L0_ffn_val_w, L0_ffn_val_b, L0_ffn_out_w, L0_ffn_out_b,
    _debug=(), _cores=None, _no_ffn=False, _ablate=(),
):
    f32 = np.float32
    h = np.asarray(L0_attn_h, f32)

    def T(w):
        return np.ascontiguousarray(np.asarray(w, f32).T)

    def bc(b):
        bb = np.asarray(b, f32)
        return np.ascontiguousarray(
            np.broadcast_to(bb[None, :], (128, bb.shape[0])))

    def col(b, n):
        return np.ascontiguousarray(np.asarray(b, f32).reshape(n, 128).T)

    shared = {
        "wdkvT": T(L0_attn_W_DKV_w),
        "bdkv_col": col(L0_attn_W_DKV_b, DC // 128),
        "wdqT": T(L0_attn_W_DQ_w), "bdq_col": col(L0_attn_W_DQ_b, DQC // 128),
        "wuqT": T(L0_attn_W_UQ_w), "buq_bc": bc(L0_attn_W_UQ_b),
        "wukT": T(L0_attn_W_UK_w), "buk_bc": bc(L0_attn_W_UK_b),
        "wuvT": T(L0_attn_W_UV_w), "buv_bc": bc(L0_attn_W_UV_b),
        "wkrT": T(L0_attn_W_KR_w), "bkr_bc": bc(L0_attn_W_KR_b),
        "woT": T(L0_attn_W_O_w), "bo_bc": bc(L0_attn_W_O_b),
        "wgT": T(L0_ffn_gate_w), "bg_bc": bc(L0_ffn_gate_b),
        "wvT": T(L0_ffn_val_w), "bv_bc": bc(L0_ffn_val_b),
        "woutT": T(L0_ffn_out_w), "bout_bc": bc(L0_ffn_out_b),
        "identI": np.eye(128, dtype=BF),
    }
    cosT, sinT = _rope_tables()
    shared["cosT"] = cosT
    shared["sinT"] = sinT

    hTb = [np.ascontiguousarray(h[b].T) for b in range(B)]

    in_maps = []
    ncores = 8 if _cores is None else _cores
    for core in range(ncores):
        b, j = core // 4, core % 4
        own = np.concatenate(
            [np.arange(128 * (j + 4 * t), 128 * (j + 4 * t) + 128)
             for t in range(NT)])
        mask = (np.arange(CH)[None, :] <= 128 * j + np.arange(128)[:, None])
        m = dict(shared)
        m["hT"] = hTb[b]
        m["hTown"] = np.ascontiguousarray(hTb[b][:, own])
        m["h_own"] = np.ascontiguousarray(h[b][own])
        m["cos_own"] = np.ascontiguousarray(cosT[own])
        m["sin_own"] = np.ascontiguousarray(sinT[own])
        m["maskI"] = mask.astype(BF)
        in_maps.append(m)

    nc = build_program(debug=_debug, no_ffn=_no_ffn, ablate=_ablate)
    res = run_bass_kernel_spmd(nc, in_maps, core_ids=list(range(ncores)))

    out = np.zeros((B, S, D), np.float32)
    for core in range(ncores):
        b, j = core // 4, core % 4
        co = res.results[core]["out"]
        for t in range(NT):
            out[b, 128 * (j + 4 * t):128 * (j + 4 * t) + 128] = \
                co[128 * t:128 * t + 128]
    if _debug:
        return out, res
    return out
